# version 02
# baseline (speedup 1.0000x reference)
import contextlib
import numpy as np

import concourse.bass as bass
import concourse.mybir as mybir
from concourse.bass_utils import run_bass_kernel_spmd

F32 = mybir.dt.float32
BF16 = mybir.dt.bfloat16
AF = mybir.ActivationFunctionType
ALU = mybir.AluOpType
AX = mybir.AxisListType

_INV_FREQ_BITS = [
    1065353216, 1066651621, 1068150997, 1069882448, 1071881897, 1073966324,
    1075299478, 1076838980, 1078616770, 1080669729, 1082585441, 1083954273,
    1085534976, 1087360344, 1089468245, 1091210726, 1092616192, 1094239199,
    1096113418, 1098277732, 1099842346, 1101285425, 1102951868, 1104876246,
    1107098482, 1108480469, 1109962169, 1111673209, 1113649088, 1115807831,
    1117125269, 1118646624, 1120403456, 1122432214, 1124424230, 1125776926,
    1127338996, 1129142846, 1131225899, 1133046725, 1134435623, 1136039498,
    1137891623, 1140030424, 1141675480, 1143101549, 1144748347, 1146650039,
    1148846080, 1150310662, 1151774896, 1153465766, 1155418353, 1157650535,
    1158952443, 1160455863, 1162191985, 1164196828, 1166264248, 1167600999,
    1169144654, 1170927240, 1172985737, 1174883987,
]

T = 4096
OWN = 2048
H = 1024
E = 2048
S = 128
UVC = 4224
FFN = 4096
NSUP = 8
QS = 4
RSQ = float(1.0 / np.sqrt(128.0))
EPS = 1e-5
BIG = 65536.0

_DEBUG = None
_LIMIT = 9
_P2S = 9
_QSN = 4
_NOZFMST = False


class SemCounter:
    def __init__(self, name):
        self.name = name
        self.h = None
        self.v = 0

    def inc(self, n):
        self.v += n
        return self.v


def build_program():
    nc = bass.Bass("TRN2", target_bir_lowering=False)

    d_x = nc.dram_tensor("x", [T, H], F32, kind="ExternalInput").ap()
    d_uvw = nc.dram_tensor("uv_w", [UVC, H], F32, kind="ExternalInput").ap()
    d_uvb_r = nc.dram_tensor("uv_b_row", [1, UVC], F32, kind="ExternalInput").ap()
    d_uvb_c = nc.dram_tensor("uv_b_col", [UVC, 1], F32, kind="ExternalInput").ap()
    d_lnw = nc.dram_tensor("ln_w", [H, 1], F32, kind="ExternalInput").ap()
    d_lnb = nc.dram_tensor("ln_b", [H, 1], F32, kind="ExternalInput").ap()
    d_g0 = nc.dram_tensor("gamma0", [S, 1], F32, kind="ExternalInput").ap()
    d_g1 = nc.dram_tensor("gamma1", [S, 1], F32, kind="ExternalInput").ap()
    d_b0 = nc.dram_tensor("beta0", [S, 1], F32, kind="ExternalInput").ap()
    d_b1 = nc.dram_tensor("beta1", [S, 1], F32, kind="ExternalInput").ap()
    d_ow = nc.dram_tensor("o_w", [H, E], F32, kind="ExternalInput").ap()
    d_ob = nc.dram_tensor("o_b", [H, 1], F32, kind="ExternalInput").ap()
    d_gw = nc.dram_tensor("gate_w", [H, 8], F32, kind="ExternalInput").ap()
    d_gb = nc.dram_tensor("gate_b", [1, 8], F32, kind="ExternalInput").ap()
    d_w1 = nc.dram_tensor("w1", [8 * H, FFN], F32, kind="ExternalInput").ap()
    d_w3 = nc.dram_tensor("w3", [8 * H, FFN], F32, kind="ExternalInput").ap()
    d_w2 = nc.dram_tensor("w2", [8 * FFN, H], F32, kind="ExternalInput").ap()
    d_sh2 = nc.dram_tensor("sh2ext", [128, 8064], F32, kind="ExternalInput").ap()
    d_sin = nc.dram_tensor("sin_tm", [T, 64], F32, kind="ExternalInput").ap()
    d_cos = nc.dram_tensor("cos_tm", [T, 64], F32, kind="ExternalInput").ap()
    d_id32 = nc.dram_tensor("idn32", [128, 128], F32, kind="ExternalInput").ap()
    d_id16 = nc.dram_tensor("idn16", [128, 128], BF16, kind="ExternalInput").ap()
    d_ones = nc.dram_tensor("ones_row", [1, 128], F32, kind="ExternalInput").ap()
    d_iota = nc.dram_tensor("iota8", [128, 8], F32, kind="ExternalInput").ap()
    d_iotab = nc.dram_tensor("iota8b", [128, 8], F32, kind="ExternalInput").ap()

    d_out = nc.dram_tensor("out", [OWN, H], F32, kind="ExternalOutput").ap()

    d_vsc = nc.dram_tensor("v_sc", [T, E], BF16, kind="Internal").ap()
    d_xnfm = nc.dram_tensor("xnfm_sc", [QS * 128, 4096], BF16, kind="Internal").ap()
    d_uct = nc.dram_tensor("ucolsT_sc", [H, E], BF16, kind="Internal").ap()
    d_owt = nc.dram_tensor("owT_sc", [E, H], BF16, kind="Internal").ap()
    d_ztm = nc.dram_tensor("ztm_sc", [OWN, H], F32, kind="Internal").ap()
    d_zfm = nc.dram_tensor("zfm_sc", [128, 8 * OWN], BF16, kind="Internal").ap()

    dbg = None
    if _DEBUG == "v":
        dbg = nc.dram_tensor("dbg", [T, E], F32, kind="ExternalOutput").ap()
    elif _DEBUG == "xnfm":
        dbg = nc.dram_tensor("dbg", [QS * 128, 4096], F32, kind="ExternalOutput").ap()
    elif _DEBUG == "z":
        dbg = nc.dram_tensor("dbg", [OWN, H], F32, kind="ExternalOutput").ap()
    elif _DEBUG == "zfm":
        dbg = nc.dram_tensor("dbg", [128, 8 * OWN], F32, kind="ExternalOutput").ap()
    elif _DEBUG == "uct":
        dbg = nc.dram_tensor("dbg", [H, E], F32, kind="ExternalOutput").ap()
    elif _DEBUG == "owt":
        dbg = nc.dram_tensor("dbg", [E, H], F32, kind="ExternalOutput").ap()
    elif _DEBUG == "y2":
        dbg = nc.dram_tensor("dbg", [OWN, H], F32, kind="ExternalOutput").ap()
    elif _DEBUG == "comb":
        dbg = nc.dram_tensor("dbg", [128, 128], F32, kind="ExternalOutput").ap()

    prog = {"sync": [], "scalar": [], "vector": [], "tensor": [], "gpsimd": []}

    def emit(eng, fn):
        prog[eng].append(fn)

    sems = []

    def new_sem(name):
        s = SemCounter(name)
        sems.append(s)
        return s

    dPE = new_sem("dPE")
    dACT = new_sem("dACT")
    dDVE = new_sem("dDVE")
    sSm = new_sem("sSm")
    sW0 = [new_sem("sW0a"), new_sem("sW0b")]
    sSt0 = new_sem("sSt0")
    sX = [new_sem("sXa"), new_sem("sXb")]
    sVs = [new_sem("sVsa"), new_sem("sVsb")]
    sXnW = new_sem("sXnW")
    sXf = new_sem("sXf")
    sX2 = new_sem("sX2")
    sUc = [new_sem("sUc0"), new_sem("sUc1")]
    sOw = [new_sem("sOw0"), new_sem("sOw1")]
    sSh = [new_sem("sSh0"), new_sem("sSh1")]
    sVq = [new_sem("sVq0"), new_sem("sVq1"), new_sem("sVq2")]
    sZW = new_sem("sZW")
    sZR = new_sem("sZR")
    sMW = [new_sem("sMW0"), new_sem("sMW1")]
    sOut = new_sem("sOut")
    sDbg = new_sem("sDbg")
    sUvb = new_sem("sUvb")
    sSc = new_sem("sSc")

    def wait(eng, sem, val):
        if val and val > 0:
            emit(eng, lambda e, s=sem, v=val: e.wait_ge(s.h, v))

    def drain(eng):
        emit(eng, lambda e: e.drain())

    def dma(eng, sem, out, in_, **kw):
        tgt = sem.inc(16)
        emit(eng, lambda e, o=out, i=in_, s=sem, kw=kw: e.dma_start(out=o, in_=i, **kw).then_inc(s.h, 16))
        return tgt

    def op(eng, method, *args, inc=None, **kw):
        if inc is not None:
            inc.inc(1)

        def f(e, m=method, a=args, kw=kw, s=inc):
            r = getattr(e, m)(*a, **kw)
            if s is not None:
                r.then_inc(s.h, 1)
        emit(eng, f)
        return None if inc is None else inc.v

    with contextlib.ExitStack() as outer:
        octx = outer.enter_context

        ones_row = octx(nc.sbuf_tensor("c_ones", [1, 128], F32))
        idn32 = octx(nc.sbuf_tensor("c_id32", [128, 128], F32))
        idn16 = octx(nc.sbuf_tensor("c_id16", [128, 128], BF16))
        iota8 = octx(nc.sbuf_tensor("c_iota", [128, 8], F32))
        iota8b = octx(nc.sbuf_tensor("c_iotab", [128, 8], F32))
        lnw_c = octx(nc.sbuf_tensor("c_lnw", [128, 8], F32))
        lnb_c = octx(nc.sbuf_tensor("c_lnb", [128, 8], F32))
        ob_c = octx(nc.sbuf_tensor("c_ob", [128, 8], F32))
        uvbu_c = octx(nc.sbuf_tensor("c_uvbu", [128, 16], F32))
        uvbs_c = octx(nc.sbuf_tensor("c_uvbs", [128, 1], F32))
        g0c = octx(nc.sbuf_tensor("c_g0", [128, 1], F32))
        b0c = octx(nc.sbuf_tensor("c_b0", [128, 1], F32))
        g1c = octx(nc.sbuf_tensor("c_g1", [128, 1], F32))
        b1c = octx(nc.sbuf_tensor("c_b1", [128, 1], F32))
        g0s = octx(nc.sbuf_tensor("c_g0s", [128, 1], F32))
        b0s = octx(nc.sbuf_tensor("c_b0s", [128, 1], F32))
        gw_sb = octx(nc.sbuf_tensor("c_gw", [128, 64], F32))
        gb_row = octx(nc.sbuf_tensor("c_gb", [1, 8], F32))
        k_fm = octx(nc.sbuf_tensor("k_fm", [128, T], BF16))
        q_fm = octx(nc.sbuf_tensor("q_fm", [128, OWN], BF16))
        combine = octx(nc.sbuf_tensor("combine", [128, 16 * 8], F32))
        stat_sum = octx(nc.sbuf_tensor("st_sum", [128, 4], F32))
        stat_sq = octx(nc.sbuf_tensor("st_sq", [128, 4], F32))
        mu4 = octx(nc.sbuf_tensor("st_mu", [128, 4], F32))
        musq4 = octx(nc.sbuf_tensor("st_musq", [128, 4], F32))
        var4 = octx(nc.sbuf_tensor("st_var", [128, 4], F32))
        rstd4 = octx(nc.sbuf_tensor("st_rstd", [128, 4], F32))
        nmr4 = octx(nc.sbuf_tensor("st_nmr", [128, 4], F32))
        sq_trash = octx(nc.sbuf_tensor("sq_trash", [128, 1024], F32))

        dma("sync", sSm, ones_row[:, :], d_ones[:, :])
        dma("sync", sSm, idn32[:, :], d_id32[:, :])
        dma("sync", sSm, idn16[:, :], d_id16[:, :])
        dma("sync", sSm, iota8[:, :], d_iota[:, :])
        dma("sync", sSm, iota8b[:, :], d_iotab[:, :])
        for fc in range(8):
            dma("sync", sSm, lnw_c[:, fc:fc + 1], d_lnw[fc * 128:(fc + 1) * 128, :])
            dma("sync", sSm, lnb_c[:, fc:fc + 1], d_lnb[fc * 128:(fc + 1) * 128, :])
            dma("sync", sSm, ob_c[:, fc:fc + 1], d_ob[fc * 128:(fc + 1) * 128, :])
        for ec in range(16):
            dma("sync", sSm, uvbu_c[:, ec:ec + 1], d_uvb_c[ec * 128:(ec + 1) * 128, :])
        dma("sync", sSm, uvbs_c[:, :], d_uvb_c[4096:4224, :])
        dma("sync", sSm, g0c[:, :], d_g0[:, :])
        dma("sync", sSm, b0c[:, :], d_b0[:, :])
        dma("sync", sSm, g1c[:, :], d_g1[:, :])
        dma("sync", sSm, b1c[:, :], d_b1[:, :])
        dma("sync", sSm, gw_sb[:, :].rearrange("p (fc e) -> p fc e", fc=8),
            d_gw.rearrange("(fc p) e -> p fc e", p=128))
        dma("sync", sSm, gb_row[:, :], d_gb[:, :])
        sm_consts = sSm.v

        wait("vector", sSm, sm_consts)
        op("vector", "tensor_scalar_mul", g0s[:, :], g0c[:, :], RSQ)
        op("vector", "tensor_scalar_mul", b0s[:, :], b0c[:, :], RSQ, inc=dDVE)
        drain("vector")

        scopeA = contextlib.ExitStack()
        uvwT_vs = scopeA.enter_context(nc.sbuf_tensor("uvwT_vs", [128, 8 * 2176], BF16))
        uvb_row = scopeA.enter_context(nc.sbuf_tensor("c_uvbr", [1, UVC], F32))
        dma("sync", sUvb, uvb_row[:, :], d_uvb_r[:, :])

        with contextlib.ExitStack() as ph0:
            p0 = ph0.enter_context
            ldbuf = [p0(nc.sbuf_tensor("p0_ld0", [128, 2048], F32)),
                     p0(nc.sbuf_tensor("p0_ld1", [128, 2048], F32))]
            u_stage = p0(nc.sbuf_tensor("p0_ustage", [128, 8 * E], BF16))
            o_stage = p0(nc.sbuf_tensor("p0_ostage", [128, 16 * H], BF16))
            pt = [p0(nc.psum_tensor("p0_pt0", [128, 1024], F32)),
                  p0(nc.psum_tensor("p0_pt1", [128, 1024], F32))]
            ld_tgt = [0, 0]
            evac_done = [0, 0]

            wait("tensor", sSm, sm_consts)
            for cc in range(33):
                sl = cc % 2
                wait("sync", dACT, evac_done[sl])
                ld_tgt[sl] = dma("sync", sW0[sl], ldbuf[sl][:, 0:1024],
                                 d_uvw[cc * 128:(cc + 1) * 128, :])
                wait("tensor", sW0[sl], ld_tgt[sl])
                wait("tensor", dACT, evac_done[sl])
                for fc in range(8):
                    op("tensor", "transpose", pt[sl][0:128, fc * 128:(fc + 1) * 128],
                       ldbuf[sl][:, fc * 128:(fc + 1) * 128], idn32[:, :],
                       inc=dPE if fc == 7 else None)
                wait("scalar", dPE, dPE.v)
                if cc < 16:
                    out_ap = u_stage[:, :].rearrange("p (fc e) -> p fc e", fc=8)[:, :, cc * 128:(cc + 1) * 128]
                elif cc < 32:
                    out_ap = uvwT_vs[:, :].rearrange("p (fc c) -> p fc c", fc=8)[:, :, (cc - 16) * 128:(cc - 15) * 128]
                else:
                    out_ap = uvwT_vs[:, :].rearrange("p (fc c) -> p fc c", fc=8)[:, :, 2048:2176]
                in_ap = pt[sl][0:128, 0:1024].rearrange("p (fc c) -> p fc c", fc=8)
                op("scalar", "activation", out_ap, in_ap, AF.Copy, inc=dACT)
                evac_done[sl] = dACT.v
            wait("sync", dACT, dACT.v)
            dma("sync", sSt0, d_uct.rearrange("(fc p) e -> p fc e", p=128),
                u_stage[:, :].rearrange("p (fc e) -> p fc e", fc=8))

            for hc in range(8):
                sl = hc % 2
                wait("sync", dACT, evac_done[sl])
                ld_tgt[sl] = dma("sync", sW0[sl], ldbuf[sl][:, 0:2048],
                                 d_ow[hc * 128:(hc + 1) * 128, :])
                wait("tensor", sW0[sl], ld_tgt[sl])
                for half in range(2):
                    wait("tensor", dACT, evac_done[sl])
                    for j in range(8):
                        ec = half * 8 + j
                        op("tensor", "transpose", pt[sl][0:128, j * 128:(j + 1) * 128],
                           ldbuf[sl][:, ec * 128:(ec + 1) * 128], idn32[:, :],
                           inc=dPE if j == 7 else None)
                    wait("scalar", dPE, dPE.v)
                    out_ap = o_stage[:, :].rearrange("p (ec h) -> p ec h", ec=16)[
                        :, half * 8:(half + 1) * 8, hc * 128:(hc + 1) * 128]
                    in_ap = pt[sl][0:128, 0:1024].rearrange("p (j c) -> p j c", j=8)
                    op("scalar", "activation", out_ap, in_ap, AF.Copy, inc=dACT)
                    evac_done[sl] = dACT.v
            wait("sync", dACT, dACT.v)
            dma("sync", sSt0, d_owt.rearrange("(ec p) h -> p ec h", p=128),
                o_stage[:, :].rearrange("p (ec h) -> p ec h", ec=16))
            st0_all = sSt0.v
            ph0_pe, ph0_act = dPE.v, dACT.v

            if _DEBUG == "uct":
                wait("gpsimd", sSt0, st0_all)
                dma("gpsimd", sDbg, dbg[:, :], d_uct[:, :])
            if _DEBUG == "owt":
                wait("gpsimd", sSt0, st0_all)
                dma("gpsimd", sDbg, dbg[:, :], d_owt[:, :])

        if _LIMIT >= 1:
            with contextlib.ExitStack() as ph1:
                p1 = ph1.enter_context
                xsup = [p1(nc.sbuf_tensor("p1_x0", [128, 4096], F32)),
                        p1(nc.sbuf_tensor("p1_x1", [128, 4096], F32))]
                xn2 = p1(nc.sbuf_tensor("p1_xn2", [128, 4096], F32))
                xnfm = p1(nc.sbuf_tensor("p1_xnfm", [128, 4096], BF16))
                vstage = [p1(nc.sbuf_tensor("p1_vs0", [128, 2048], BF16)),
                          p1(nc.sbuf_tensor("p1_vs1", [128, 2048], BF16))]
                basek = p1(nc.sbuf_tensor("p1_basek", [128, 512], F32))
                kb = p1(nc.sbuf_tensor("p1_kb", [128, 512], F32))
                kb_tm = p1(nc.sbuf_tensor("p1_kbtm", [128, 512], F32))
                krp = p1(nc.sbuf_tensor("p1_krp", [128, 512], BF16))
                m1 = p1(nc.sbuf_tensor("p1_m1", [128, 64], F32))
                m2 = p1(nc.sbuf_tensor("p1_m2", [128, 64], F32))
                m3 = p1(nc.sbuf_tensor("p1_m3", [128, 64], F32))
                m4 = p1(nc.sbuf_tensor("p1_m4", [128, 64], F32))
                sin_tm = p1(nc.sbuf_tensor("p1_sin", [128, 32 * 64], F32))
                cos_tm = p1(nc.sbuf_tensor("p1_cos", [128, 32 * 64], F32))
                ptr = [p1(nc.psum_tensor("p1_ptr0", [128, 512], F32)),
                       p1(nc.psum_tensor("p1_ptr1", [128, 512], F32))]
                pvs = [p1(nc.psum_tensor("p1_pv0", [128, 512], F32)),
                       p1(nc.psum_tensor("p1_pv1", [128, 512], F32))]
                pkq = [p1(nc.psum_tensor("p1_pk0", [128, 512], F32)),
                       p1(nc.psum_tensor("p1_pk1", [128, 512], F32))]
                pbf = p1(nc.psum_tensor("p1_pbf", [128, 1024], BF16))

                wait("sync", sSt0, st0_all)
                wait("sync", dPE, ph0_pe)
                dma("sync", sSc, sin_tm[:, :].rearrange("p (t f) -> p t f", f=64),
                    d_sin.rearrange("(t p) f -> p t f", p=128))
                dma("sync", sSc, cos_tm[:, :].rearrange("p (t f) -> p t f", f=64),
                    d_cos.rearrange("(t p) f -> p t f", p=128))
                wait("vector", sSc, sSc.v)
                wait("vector", sSm, sm_consts)
                wait("tensor", sSm, sm_consts)
                wait("tensor", sUvb, sUvb.v)

                x_tgt = [0, 0]
                x_lastread = [0, 0]
                vst_tgt = [0, 0]
                vev_done = {}
                vev_lin = 0
                xnfm_lastpe = 0
                kq_evac = [0, 0]
                basek_lastdve = 0
                kbtm_lastdve = 0
                krp_lastpe = 0
                pbf_lastread = 0

                for s in range(NSUP):
                    sl = s % 2
                    wait("sync", dDVE, x_lastread[sl])
                    x_tgt[sl] = dma("sync", sX[sl],
                                    xsup[sl][:, :].rearrange("p (tc f) -> p tc f", f=H),
                                    d_x.rearrange("(tc p) f -> p tc f", p=128)[:, s * 4:(s + 1) * 4, :])
                    wait("scalar", sX[sl], x_tgt[sl])
                    for tc in range(4):
                        if tc:
                            drain("scalar")
                        op("scalar", "activation", sq_trash[:, :], xsup[sl][:, tc * H:(tc + 1) * H],
                           AF.Square, accum_out=stat_sq[:, tc:tc + 1],
                           inc=dACT if tc == 3 else None)
                    sq_done = dACT.v
                    wait("vector", sX[sl], x_tgt[sl])
                    for tc in range(4):
                        op("vector", "reduce_sum", stat_sum[:, tc:tc + 1],
                           xsup[sl][:, tc * H:(tc + 1) * H], AX.X)
                    drain("vector")
                    op("vector", "tensor_scalar_mul", mu4[:, :], stat_sum[:, :], 1.0 / H)
                    drain("vector")
                    op("vector", "tensor_mul", musq4[:, :], mu4[:, :], mu4[:, :])
                    drain("vector")
                    wait("vector", dACT, sq_done)
                    op("vector", "tensor_scalar", var4[:, :], stat_sq[:, :], 1.0 / H, None, ALU.mult)
                    drain("vector")
                    op("vector", "tensor_sub", var4[:, :], var4[:, :], musq4[:, :])
                    drain("vector")
                    op("vector", "tensor_scalar", var4[:, :], var4[:, :], float(EPS), None, ALU.add,
                       inc=dDVE)
                    wait("scalar", dDVE, dDVE.v)
                    op("scalar", "activation", var4[:, :], var4[:, :], AF.Ln)
                    drain("scalar")
                    op("scalar", "activation", rstd4[:, :], var4[:, :], AF.Exp, scale=-0.5,
                       inc=dACT)
                    wait("vector", dACT, dACT.v)
                    op("vector", "scalar_tensor_tensor", nmr4[:, :], mu4[:, :], -1.0, rstd4[:, :],
                       op0=ALU.mult, op1=ALU.mult)
                    drain("vector")
                    for tc in range(4):
                        op("vector", "tensor_scalar", xn2[:, tc * H:(tc + 1) * H],
                           xsup[sl][:, tc * H:(tc + 1) * H],
                           rstd4[:, tc:tc + 1], nmr4[:, tc:tc + 1], ALU.mult, ALU.add,
                           inc=dDVE if tc == 3 else None)
                    xn2_done = dDVE.v
                    x_lastread[sl] = xn2_done

                    ln_aff = [0] * 8
                    wait("tensor", dDVE, xn2_done)
                    wait("scalar", dPE, xnfm_lastpe)
                    wait("scalar", sXnW, sXnW.v)
                    for fc in range(8):
                        tsl = fc % 2
                        if fc >= 2:
                            wait("tensor", dACT, ln_aff[fc - 2])
                        for tc in range(4):
                            op("tensor", "transpose", ptr[tsl][0:128, tc * 128:(tc + 1) * 128],
                               xn2[:, tc * H + fc * 128:tc * H + (fc + 1) * 128], idn32[:, :],
                               inc=dPE if tc == 3 else None)
                        wait("scalar", dPE, dPE.v)
                        op("scalar", "activation", xnfm[:, fc * 512:(fc + 1) * 512],
                           ptr[tsl][0:128, 0:512], AF.Identity,
                           scale=lnw_c[:, fc:fc + 1], bias=lnb_c[:, fc:fc + 1], inc=dACT)
                        ln_aff[fc] = dACT.v
                    xnfm_done = dACT.v
                    if s < QS:
                        wait("sync", dACT, xnfm_done)
                        dma("sync", sXnW, d_xnfm[s * 128:(s + 1) * 128, :], xnfm[:, :])

                    wait("tensor", dACT, xnfm_done)
                    for tc in range(4):
                        vsl = tc % 2
                        for eg in range(4):
                            psl = vev_lin % 2
                            wait("tensor", dACT, vev_done.get(vev_lin - 2, 0))
                            op("tensor", "matmul", pvs[psl][0:128, :],
                               ones_row[0:1, 0:128], uvb_row[0:1, E + eg * 512:E + (eg + 1) * 512],
                               start=True, stop=False)
                            for fc in range(8):
                                op("tensor", "matmul", pvs[psl][0:128, :],
                                   xnfm[:, fc * 512 + tc * 128:fc * 512 + (tc + 1) * 128],
                                   uvwT_vs[:, fc * 2176 + eg * 512:fc * 2176 + (eg + 1) * 512],
                                   start=False, stop=(fc == 7),
                                   inc=dPE if fc == 7 else None)
                            wait("scalar", dPE, dPE.v)
                            if eg == 0:
                                wait("scalar", sVs[vsl], vst_tgt[vsl])
                            op("scalar", "activation", vstage[vsl][:, eg * 512:(eg + 1) * 512],
                               pvs[psl][0:128, :], AF.Silu, inc=dACT)
                            vev_done[vev_lin] = dACT.v
                            vev_lin += 1
                        wait("sync", dACT, dACT.v)
                        tcg = s * 4 + tc
                        vst_tgt[vsl] = dma("sync", sVs[vsl],
                                           d_vsc[tcg * 128:(tcg + 1) * 128, :], vstage[vsl][:, :])

                    for which in (["k", "q"] if s < QS else ["k"]):
                        ksl = 0 if which == "k" else 1
                        wait("tensor", dACT, kq_evac[ksl])
                        for fc in range(8):
                            op("tensor", "matmul", pkq[ksl][0:128, :],
                               uvwT_vs[:, fc * 2176 + 2048:fc * 2176 + 2176],
                               xnfm[:, fc * 512:(fc + 1) * 512],
                               start=(fc == 0), stop=(fc == 7),
                               inc=dPE if fc == 7 else None)
                        wait("scalar", dPE, dPE.v)
                        wait("scalar", dDVE, basek_lastdve)
                        op("scalar", "activation", basek[:, :], pkq[ksl][0:128, :], AF.Silu,
                           bias=uvbs_c[:, :], inc=dACT)
                        kq_evac[ksl] = dACT.v
                        wait("vector", dACT, dACT.v)
                        gc, bc = (g1c, b1c) if which == "k" else (g0s, b0s)
                        op("vector", "tensor_scalar", kb[:, :], basek[:, :],
                           gc[:, :], bc[:, :], ALU.mult, ALU.add, inc=dDVE)
                        kb_done = dDVE.v
                        basek_lastdve = kb_done
                        wait("tensor", dDVE, kb_done)
                        for tc in range(4):
                            op("tensor", "transpose", pkq[ksl][0:128, tc * 128:(tc + 1) * 128],
                               kb[:, tc * 128:(tc + 1) * 128], idn32[:, :],
                               inc=dPE if tc == 3 else None)
                        wait("scalar", dPE, dPE.v)
                        wait("scalar", dDVE, kbtm_lastdve)
                        op("scalar", "activation", kb_tm[:, :], pkq[ksl][0:128, :], AF.Copy,
                           inc=dACT)
                        kq_evac[ksl] = dACT.v
                        wait("vector", dACT, dACT.v)
                        wait("vector", dPE, krp_lastpe)
                        for tc in range(4):
                            t0 = tc * 128
                            so = (s * 4 + tc) * 64
                            if tc:
                                drain("vector")
                            op("vector", "tensor_mul", m1[:, :], kb_tm[:, t0:t0 + 64], cos_tm[:, so:so + 64])
                            op("vector", "tensor_mul", m2[:, :], kb_tm[:, t0 + 64:t0 + 128], sin_tm[:, so:so + 64])
                            op("vector", "tensor_mul", m3[:, :], kb_tm[:, t0 + 64:t0 + 128], cos_tm[:, so:so + 64])
                            op("vector", "tensor_mul", m4[:, :], kb_tm[:, t0:t0 + 64], sin_tm[:, so:so + 64])
                            drain("vector")
                            op("vector", "tensor_sub", krp[:, t0:t0 + 64], m1[:, :], m2[:, :])
                            op("vector", "tensor_add", krp[:, t0 + 64:t0 + 128], m3[:, :], m4[:, :],
                               inc=dDVE if tc == 3 else None)
                        krp_done = dDVE.v
                        kbtm_lastdve = krp_done
                        wait("tensor", dDVE, krp_done)
                        wait("tensor", dDVE, pbf_lastread)
                        for tc in range(4):
                            op("tensor", "transpose", pbf[0:128, tc * 128:(tc + 1) * 128],
                               krp[:, tc * 128:(tc + 1) * 128], idn16[:, :],
                               inc=dPE if tc == 3 else None)
                        krp_lastpe = dPE.v
                        wait("vector", dPE, dPE.v)
                        dst = k_fm[:, s * 512:(s + 1) * 512] if which == "k" else q_fm[:, s * 512:(s + 1) * 512]
                        op("vector", "tensor_copy", dst, pbf[0:128, 0:512], inc=dDVE)
                        pbf_lastread = dDVE.v
                    xnfm_lastpe = dPE.v

                ph1_pe, ph1_dve, ph1_act = dPE.v, dDVE.v, dACT.v
                vst_all = [sVs[0].v, sVs[1].v]
                xnw_all = sXnW.v

                if _DEBUG == "xnfm":
                    wait("gpsimd", sXnW, xnw_all)
                    dma("gpsimd", sDbg, dbg[:, :], d_xnfm[:, :])
                if _DEBUG == "v":
                    wait("gpsimd", sVs[0], vst_all[0])
                    wait("gpsimd", sVs[1], vst_all[1])
                    dma("gpsimd", sDbg, dbg[:, :], d_vsc[:, :])

        scopeA.close()

        if _LIMIT >= 2:
            SH2_C0 = 1920

            def sh_off(jc, qs):
                o = SH2_C0 - jc * 128 + qs * 512 + (6144 if jc >= 16 else 0)
                assert 0 <= o <= 8064 - 512
                return o

            with contextlib.ExitStack() as ph2:
                p2 = ph2.enter_context
                scT = p2(nc.sbuf_tensor("p2_scT", [128, 32 * 512], BF16))
                uT = p2(nc.sbuf_tensor("p2_uT", [128, 16 * 512], BF16))
                ucb = [p2(nc.sbuf_tensor("p2_uc0", [128, 8 * 512], BF16)),
                       p2(nc.sbuf_tensor("p2_uc1", [128, 8 * 512], BF16))]
                owb = [p2(nc.sbuf_tensor("p2_ow0", [128, 1024], BF16)),
                       p2(nc.sbuf_tensor("p2_ow1", [128, 1024], BF16))]
                shb = [p2(nc.sbuf_tensor("p2_sh0", [128, 512], F32)),
                       p2(nc.sbuf_tensor("p2_sh1", [128, 512], F32))]
                sc32 = [p2(nc.sbuf_tensor("p2_s0", [128, 512], F32)),
                        p2(nc.sbuf_tensor("p2_s1", [128, 512], F32))]
                rl32 = [p2(nc.sbuf_tensor("p2_r0", [128, 512], F32)),
                        p2(nc.sbuf_tensor("p2_r1", [128, 512], F32))]
                xnfm2 = p2(nc.sbuf_tensor("p2_xnfm", [128, 4096], BF16))
                z32t = p2(nc.sbuf_tensor("p2_z0", [128, 512], F32))
                pb = [p2(nc.psum_tensor(f"p2_pb{i}", [128, 512], F32)) for i in range(8)]

                pb_free = {i: [] for i in range(8)}

                def pb_wait(i):
                    for sem, v in pb_free[i]:
                        if v > 0:
                            emit("tensor", lambda e, s=sem, vv=v: e.wait_ge(s.h, vv))
                    pb_free[i] = []

                for engn in ("sync", "scalar"):
                    wait(engn, sXnW, xnw_all)
                    wait(engn, sSt0, st0_all)
                    wait(engn, sVs[0], vst_all[0])
                    wait(engn, sVs[1], vst_all[1])
                    wait(engn, dPE, ph1_pe)
                    wait(engn, dDVE, ph1_dve)
                    wait(engn, dACT, ph1_act)

                uT_lastpe = 0
                scT_lastpe = 0
                xnfm2_read = 0
                xs2_read = 0
                vq_last = {0: 0, 1: 0, 2: 0}
                uc_tgt = [0, 0]
                uc_lastpe = [0, 0]
                ow_tgt = [0, 0]
                ow_lastpe = [0, 0]
                sh_tgt = [0, 0]
                prev_gate_mm = 0
                prev_zfmevac = 0
                combine_done = 0

                for qs in range(QS):
                    wait("sync", dPE, xnfm2_read)
                    xf_tgt = dma("sync", sXf, xnfm2[:, :], d_xnfm[qs * 128:(qs + 1) * 128, :])

                    with contextlib.ExitStack() as svs:
                        vbuf = [svs.enter_context(nc.sbuf_tensor(f"p2v{i}_{qs}", [128, 16 * 512], BF16))
                                for i in range(3)]
                        wait("sync", dPE, prev_gate_mm)
                        wait("sync", dDVE, prev_zfmevac)
                        wait("sync", dDVE, combine_done)
                        wait("sync", sZW, sZW.v)

                        for eb in range(4):
                            usl = eb % 2
                            wait("scalar", dPE, uc_lastpe[usl])
                            uc_tgt[usl] = dma("scalar", sUc[usl],
                                              ucb[usl][:, :].rearrange("p (fc e) -> p fc e", fc=8),
                                              d_uct.rearrange("(fc p) e -> p fc e", p=128)[:, :, eb * 512:(eb + 1) * 512])
                            wait("tensor", sUc[usl], uc_tgt[usl])
                            if eb == 0:
                                wait("tensor", sXf, xf_tgt)
                            for e4 in range(4):
                                ec = eb * 4 + e4
                                psl = ec % 2
                                pb_wait(psl)
                                if ec == 0:
                                    wait("tensor", dPE, uT_lastpe)
                                for fc in range(8):
                                    op("tensor", "matmul", pb[psl][0:128, :],
                                       ucb[usl][:, fc * 512 + e4 * 128:fc * 512 + (e4 + 1) * 128],
                                       xnfm2[:, fc * 512:(fc + 1) * 512],
                                       start=(fc == 0), stop=(fc == 7),
                                       inc=dPE if fc == 7 else None)
                                wait("scalar", dPE, dPE.v)
                                op("scalar", "activation", uT[:, ec * 512:(ec + 1) * 512],
                                   pb[psl][0:128, :], AF.Silu, bias=uvbu_c[:, ec:ec + 1], inc=dACT)
                                pb_free[psl] = [(dACT, dACT.v)]
                            uc_lastpe[usl] = dPE.v
                        uT_done_act = dACT.v
                        xnfm2_read = dPE.v

                    if _P2S >= 2:
                            add_done = [0] * 32
                            relu_done = [0] * 32
                            for jc in range(32):
                                psl = jc % 2
                                if jc >= 2:
                                    wait("scalar", dDVE, add_done[jc - 2])
                                sh_tgt[psl] = dma("scalar", sSh[psl], shb[psl][:, :],
                                                  d_sh2[:, sh_off(jc, qs):sh_off(jc, qs) + 512])
                                pb_wait(psl)
                                op("tensor", "matmul", pb[psl][0:128, :],
                                   k_fm[:, jc * 128:(jc + 1) * 128], q_fm[:, qs * 512:(qs + 1) * 512],
                                   start=True, stop=True, inc=dPE)
                                mm_v = dPE.v
                                wait("vector", dPE, mm_v)
                                wait("vector", sSh[psl], sh_tgt[psl])
                                if jc >= 2:
                                    wait("vector", dACT, relu_done[jc - 2])
                                if jc == 0:
                                    wait("vector", dPE, scT_lastpe)
                                op("vector", "tensor_add", sc32[psl][:, :], pb[psl][0:128, :],
                                   shb[psl][:, :], inc=dDVE)
                                add_done[jc] = dDVE.v
                                pb_free[psl] = [(dDVE, dDVE.v)]
                                wait("scalar", dDVE, add_done[jc])
                                op("scalar", "activation", rl32[psl][:, :], sc32[psl][:, :], AF.Relu,
                                   inc=dACT)
                                relu_done[jc] = dACT.v
                                wait("vector", dACT, relu_done[jc])
                                op("vector", "tensor_mul", scT[:, jc * 512:(jc + 1) * 512],
                                   sc32[psl][:, :], rl32[psl][:, :], inc=dDVE)
                            scores_done = dDVE.v

                            gate_done = [0] * 16
                            for qd in range(4):
                                for jh in range(2):
                                    slot = (2 * qd + jh) % 3
                                    wait("sync", dPE, vq_last[slot])
                                    dma("sync", sVq[slot],
                                        vbuf[slot][:, :].rearrange("p (jc e) -> p jc e", jc=16),
                                        d_vsc.rearrange("(jc p) e -> p jc e", p=128)[
                                            :, jh * 16:(jh + 1) * 16, qd * 512:(qd + 1) * 512])
                                for e4 in range(4):
                                    ec = qd * 4 + e4
                                    psl = 2 + ec % 2
                                    pb_wait(psl)
                                    if e4 == 0 and qd == 0:
                                        wait("tensor", dDVE, scores_done)
                                        wait("tensor", dACT, uT_done_act)
                                    for jc in range(32):
                                        jh = jc // 16
                                        slot = (2 * qd + jh) % 3
                                        if jc % 16 == 0 and e4 == 0:
                                            wait("tensor", sVq[slot], sVq[slot].v)
                                        op("tensor", "matmul", pb[psl][0:128, :],
                                           vbuf[slot][:, (jc % 16) * 512 + e4 * 128:(jc % 16) * 512 + (e4 + 1) * 128],
                                           scT[:, jc * 512:(jc + 1) * 512],
                                           start=(jc == 0), stop=(jc == 31),
                                           inc=dPE if jc == 31 else None)
                                    wait("vector", dPE, dPE.v)
                                    op("vector", "tensor_tensor", uT[:, ec * 512:(ec + 1) * 512],
                                       pb[psl][0:128, :], uT[:, ec * 512:(ec + 1) * 512], ALU.mult,
                                       inc=dDVE)
                                    gate_done[ec] = dDVE.v
                                    pb_free[psl] = [(dDVE, dDVE.v)]
                                vq_last[(2 * qd) % 3] = dPE.v
                                vq_last[(2 * qd + 1) % 3] = dPE.v
                            scT_lastpe = dPE.v

                    if _P2S >= 4:
                        with contextlib.ExitStack() as sys_:
                            y1_fm = sys_.enter_context(nc.sbuf_tensor(f"p2y1f_{qs}", [128, 8 * 512], F32))
                            y1_tm = sys_.enter_context(nc.sbuf_tensor(f"p2y1t_{qs}", [128, 4096], F32))
                            zfm_sb = sys_.enter_context(nc.sbuf_tensor(f"p2zf_{qs}", [128, 4096], BF16))
                            xsup2 = sys_.enter_context(nc.sbuf_tensor(f"p2xs_{qs}", [128, 4096], F32))
                            wait("sync", dPE, scT_lastpe)
                            wait("sync", dDVE, xs2_read)
                            x2_tgt = dma("sync", sX2, xsup2[:, :].rearrange("p (tc f) -> p tc f", f=H),
                                         d_x.rearrange("(tc p) f -> p tc f", p=128)[:, qs * 4:(qs + 1) * 4, :])

                            for ec in range(16):
                                osl = ec % 2
                                wait("scalar", dPE, ow_lastpe[osl])
                                ow_tgt[osl] = dma("scalar", sOw[osl], owb[osl][:, :],
                                                  d_owt[ec * 128:(ec + 1) * 128, :])
                                wait("tensor", sOw[osl], ow_tgt[osl])
                                wait("tensor", dDVE, gate_done[ec])
                                if ec == 0:
                                    for i in range(8):
                                        pb_wait(i)
                                for hc in range(8):
                                    op("tensor", "matmul", pb[hc][0:128, :],
                                       owb[osl][:, hc * 128:(hc + 1) * 128],
                                       uT[:, ec * 512:(ec + 1) * 512],
                                       start=(ec == 0), stop=(ec == 15),
                                       inc=dPE if hc == 7 else None)
                                ow_lastpe[osl] = dPE.v
                            oproj_done = dPE.v
                            uT_lastpe = oproj_done

                            wait("scalar", dPE, oproj_done)
                            for hc in range(8):
                                op("scalar", "activation", y1_fm[:, hc * 512:(hc + 1) * 512],
                                   pb[hc][0:128, :], AF.Identity, bias=ob_c[:, hc:hc + 1],
                                   inc=dACT if hc == 7 else None)
                            y1_evac = dACT.v
                            for i in range(8):
                                pb_free[i] = [(dACT, y1_evac)]

                        if _P2S >= 5:
                                z_aff = [0] * 4
                                for tc in range(4):
                                    pA = 2 * (tc % 2)
                                    pB = pA + 1
                                    pb_wait(pA)
                                    pb_wait(pB)
                                    if tc == 0:
                                        wait("tensor", dACT, y1_evac)
                                    for hc in range(8):
                                        tgt = pb[pA] if hc < 4 else pb[pB]
                                        col = (hc % 4) * 128
                                        op("tensor", "transpose", tgt[0:128, col:col + 128],
                                           y1_fm[:, hc * 512 + tc * 128:hc * 512 + (tc + 1) * 128],
                                           idn32[:, :], inc=dPE if hc == 7 else None)
                                    tr_v = dPE.v
                                    wait("vector", dPE, tr_v)
                                    wait("vector", sX2, x2_tgt)
                                    op("vector", "tensor_add", y1_tm[:, tc * H:tc * H + 512],
                                       pb[pA][0:128, :], xsup2[:, tc * H:tc * H + 512])
                                    op("vector", "tensor_add", y1_tm[:, tc * H + 512:(tc + 1) * H],
                                       pb[pB][0:128, :], xsup2[:, tc * H + 512:(tc + 1) * H], inc=dDVE)
                                    y1tm_done = dDVE.v
                                    pb_free[pA] = [(dDVE, y1tm_done)]
                                    pb_free[pB] = [(dDVE, y1tm_done)]
                                    wait("scalar", dDVE, y1tm_done)
                                    op("scalar", "activation", sq_trash[:, :], y1_tm[:, tc * H:(tc + 1) * H],
                                       AF.Square, accum_out=stat_sq[:, tc:tc + 1], inc=dACT)
                                    sq_done = dACT.v
                                    drain("vector")
                                    op("vector", "reduce_sum", stat_sum[:, tc:tc + 1],
                                       y1_tm[:, tc * H:(tc + 1) * H], AX.X)
                                    drain("vector")
                                    op("vector", "tensor_scalar_mul", mu4[:, tc:tc + 1], stat_sum[:, tc:tc + 1], 1.0 / H)
                                    drain("vector")
                                    op("vector", "tensor_mul", musq4[:, tc:tc + 1], mu4[:, tc:tc + 1], mu4[:, tc:tc + 1])
                                    drain("vector")
                                    wait("vector", dACT, sq_done)
                                    op("vector", "tensor_scalar", var4[:, tc:tc + 1], stat_sq[:, tc:tc + 1],
                                       1.0 / H, None, ALU.mult)
                                    drain("vector")
                                    op("vector", "tensor_sub", var4[:, tc:tc + 1], var4[:, tc:tc + 1], musq4[:, tc:tc + 1])
                                    drain("vector")
                                    op("vector", "tensor_scalar", var4[:, tc:tc + 1], var4[:, tc:tc + 1],
                                       float(EPS), None, ALU.add, inc=dDVE)
                                    wait("scalar", dDVE, dDVE.v)
                                    op("scalar", "activation", var4[:, tc:tc + 1], var4[:, tc:tc + 1], AF.Ln)
                                    drain("scalar")
                                    op("scalar", "activation", rstd4[:, tc:tc + 1], var4[:, tc:tc + 1],
                                       AF.Exp, scale=-0.5, inc=dACT)
                                    wait("vector", dACT, dACT.v)
                                    op("vector", "scalar_tensor_tensor", nmr4[:, tc:tc + 1], mu4[:, tc:tc + 1],
                                       -1.0, rstd4[:, tc:tc + 1], op0=ALU.mult, op1=ALU.mult, inc=dDVE)
                                    wait("scalar", dDVE, dDVE.v)
                                    op("scalar", "activation", y1_tm[:, tc * H:(tc + 1) * H],
                                       y1_tm[:, tc * H:(tc + 1) * H], AF.Identity,
                                       scale=rstd4[:, tc:tc + 1], bias=nmr4[:, tc:tc + 1], inc=dACT)
                                    z_aff[tc] = dACT.v
                                    wait("sync", dACT, z_aff[tc])
                                    tt = qs * 4 + tc
                                    dma("sync", sZW, d_ztm[tt * 128:(tt + 1) * 128, :],
                                        y1_tm[:, tc * H:(tc + 1) * H])
                                xs2_read = dDVE.v

                        zf_ev_v = [0] * 8
                        zf_ev_a = [0] * 8
                        if _P2S >= 6:
                            for fc in range(8):
                                psl = 4 + fc % 2
                                pb_wait(psl)
                                if fc == 0:
                                    wait("tensor", dACT, z_aff[3])
                                for tc in range(4):
                                    op("tensor", "transpose", pb[psl][0:128, tc * 128:(tc + 1) * 128],
                                       y1_tm[:, tc * H + fc * 128:tc * H + (fc + 1) * 128], idn32[:, :],
                                       inc=dPE if tc == 3 else None)
                                trz_v = dPE.v
                                wait("vector", dPE, trz_v)
                                op("vector", "tensor_copy", zfm_sb[:, fc * 512:(fc + 1) * 512],
                                   pb[psl][0:128, :], inc=dDVE)
                                zf_ev_v[fc] = dDVE.v
                                wait("scalar", dDVE, zf_ev_v[fc])
                                op("scalar", "activation", y1_fm[:, fc * 512:(fc + 1) * 512],
                                   pb[psl][0:128, :], AF.Copy, inc=dACT)
                                zf_ev_a[fc] = dACT.v
                                pb_free[psl] = [(dDVE, zf_ev_v[fc]), (dACT, zf_ev_a[fc])]
                            wait("sync", dDVE, zf_ev_v[7])
                            prev_zfmevac = zf_ev_v[7]
                            if not _NOZFMST:
                                dma("sync", sZW,
                                    d_zfm.rearrange("p (fc t) -> p fc t", fc=8)[:, :, qs * 512:(qs + 1) * 512],
                                    zfm_sb[:, :].rearrange("p (fc t) -> p fc t", fc=8))

                        gate_mm = 0
                        if _P2S >= 7:
                            wait("tensor", dACT, zf_ev_a[7])
                            pb_wait(6)
                            for fc in range(8):
                                for tc in range(4):
                                    op("tensor", "matmul", pb[6][0:128, tc * 8:(tc + 1) * 8],
                                       y1_fm[:, fc * 512 + tc * 128:fc * 512 + (tc + 1) * 128],
                                       gw_sb[:, fc * 8:(fc + 1) * 8],
                                       start=(fc == 0 and tc == 0), stop=False)
                            for tc in range(4):
                                op("tensor", "matmul", pb[6][0:128, tc * 8:(tc + 1) * 8],
                                   ones_row[0:1, 0:128], gb_row[0:1, 0:8],
                                   start=False, stop=(tc == 3), inc=dPE if tc == 3 else None)
                            gate_mm = dPE.v
                            prev_gate_mm = gate_mm

                        if _P2S >= 8:
                                c_mx = mu4
                                c_t = musq4
                                e8 = sq_trash
                                for tc in range(4):
                                    tt = qs * 4 + tc
                                    pg = pb[6][0:128, tc * 8:(tc + 1) * 8]
                                    wait("vector", dPE, gate_mm)
                                    op("vector", "tensor_reduce", c_mx[:, 0:1], pg, AX.X, ALU.max)
                                    drain("vector")
                                    op("vector", "tensor_scalar_mul", c_mx[:, 1:2], c_mx[:, 0:1], -1.0,
                                       inc=dDVE)
                                    wait("scalar", dDVE, dDVE.v)
                                    op("scalar", "activation", e8[:, 0:8], pg, AF.Exp,
                                       bias=c_mx[:, 1:2], accum_out=c_mx[:, 2:3], inc=dACT)
                                    wait("vector", dACT, dACT.v)
                                    op("vector", "tensor_reduce", c_t[:, 0:1], e8[:, 0:8], AX.X, ALU.max)
                                    drain("vector")
                                    op("vector", "tensor_single_scalar", e8[:, 8:16], e8[:, 0:8],
                                       c_t[:, 0:1], ALU.is_equal)
                                    drain("vector")
                                    op("vector", "scalar_tensor_tensor", e8[:, 16:24], e8[:, 8:16], -BIG,
                                       iota8b[:, :], op0=ALU.mult, op1=ALU.add)
                                    drain("vector")
                                    op("vector", "tensor_reduce", c_t[:, 1:2], e8[:, 16:24], AX.X, ALU.min)
                                    drain("vector")
                                    op("vector", "tensor_single_scalar", e8[:, 24:32], iota8[:, :],
                                       c_t[:, 1:2], ALU.is_equal)
                                    drain("vector")
                                    op("vector", "scalar_tensor_tensor", e8[:, 32:40], e8[:, 24:32], -BIG,
                                       e8[:, 0:8], op0=ALU.mult, op1=ALU.add)
                                    drain("vector")
                                    op("vector", "tensor_reduce", c_t[:, 2:3], e8[:, 32:40], AX.X, ALU.max)
                                    drain("vector")
                                    op("vector", "tensor_single_scalar", e8[:, 40:48], e8[:, 32:40],
                                       c_t[:, 2:3], ALU.is_equal)
                                    drain("vector")
                                    op("vector", "scalar_tensor_tensor", e8[:, 48:56], e8[:, 40:48], -BIG,
                                       iota8b[:, :], op0=ALU.mult, op1=ALU.add)
                                    drain("vector")
                                    op("vector", "tensor_reduce", c_t[:, 3:4], e8[:, 48:56], AX.X, ALU.min)
                                    drain("vector")
                                    op("vector", "tensor_single_scalar", e8[:, 56:64], iota8[:, :],
                                       c_t[:, 3:4], ALU.is_equal)
                                    drain("vector")
                                    op("vector", "tensor_add", e8[:, 64:72], e8[:, 24:32], e8[:, 56:64])
                                    drain("vector")
                                    op("vector", "tensor_mul", e8[:, 72:80], e8[:, 0:8], e8[:, 64:72])
                                    drain("vector")
                                    op("vector", "reduce_sum", c_mx[:, 3:4], e8[:, 72:80], AX.X)
                                    drain("vector")
                                    op("vector", "reciprocal", c_t[:, 0:1], c_mx[:, 3:4])
                                    drain("vector")
                                    op("vector", "tensor_scalar_mul", combine[:, tt * 8:(tt + 1) * 8],
                                       e8[:, 72:80], c_t[:, 0:1], inc=dDVE)
                                combine_done = dDVE.v
                                pb_free[6] = [(dDVE, combine_done)]

                zW_all = sZW.v
                ph2_pe, ph2_dve, ph2_act = dPE.v, dDVE.v, dACT.v

                if _DEBUG == "z":
                    wait("sync", sZW, zW_all)
                    dma("sync", sDbg, dbg[:, :], d_ztm[:, :])
                if _DEBUG == "zfm":
                    wait("sync", sZW, zW_all)
                    dma("sync", sDbg, dbg[:, :], d_zfm[:, :])

        if _LIMIT >= 3:
            with contextlib.ExitStack() as ph3:
                p3 = ph3.enter_context
                zfm3 = p3(nc.sbuf_tensor("p3_zfm", [128, 8 * OWN], BF16))
                y2 = p3(nc.sbuf_tensor("p3_y2", [128, 16 * H], F32))
                w1b = [p3(nc.sbuf_tensor("p3_w1a", [128, 4096], BF16)),
                       p3(nc.sbuf_tensor("p3_w1b", [128, 4096], BF16))]
                w3b = [p3(nc.sbuf_tensor("p3_w3a", [128, 4096], BF16)),
                       p3(nc.sbuf_tensor("p3_w3b", [128, 4096], BF16))]
                w2b = [p3(nc.sbuf_tensor("p3_w2a", [128, 4096], BF16)),
                       p3(nc.sbuf_tensor("p3_w2b", [128, 4096], BF16))]
                ht = [p3(nc.sbuf_tensor("p3_ht0", [128, 256], F32)),
                      p3(nc.sbuf_tensor("p3_ht1", [128, 256], F32))]
                hbb = [p3(nc.sbuf_tensor("p3_hb0", [128, 256], BF16)),
                       p3(nc.sbuf_tensor("p3_hb1", [128, 256], BF16))]
                ph1_ = [p3(nc.psum_tensor("p3_h1a", [128, 512], F32)),
                        p3(nc.psum_tensor("p3_h1b", [128, 512], F32))]
                ph3_ = [p3(nc.psum_tensor("p3_h3a", [128, 512], F32)),
                        p3(nc.psum_tensor("p3_h3b", [128, 512], F32))]
                poe = [p3(nc.psum_tensor("p3_oe0", [128, 1024], F32)),
                       p3(nc.psum_tensor("p3_oe1", [128, 1024], F32))]

                for engn in ("sync", "gpsimd"):
                    wait(engn, sZW, zW_all)
                    wait(engn, dPE, ph2_pe)
                    wait(engn, dDVE, ph2_dve)
                    wait(engn, dACT, ph2_act)
                dma("sync", sZR, zfm3[:, :], d_zfm[:, :])
                dma("sync", sZR, y2[:, :].rearrange("p (tt d) -> p tt d", d=H),
                    d_ztm.rearrange("(tt p) d -> p tt d", p=128))
                zr_all = sZR.v

                groupdone = [0] * 64
                mw_tgt = [0, 0]
                hmul_done = {}
                w2_done = {}
                u_lin = 0
                oeevac_done = 0
                first = True

                for idx in range(64):
                    e_, g_ = idx // 8, idx % 8
                    par = idx % 2
                    if idx >= 2:
                        wait("gpsimd", dPE, groupdone[idx - 2])
                    dma("gpsimd", sMW[par],
                        w1b[par][:, :].rearrange("p (dc f) -> p dc f", dc=8),
                        d_w1.rearrange("(r p) f -> p r f", p=128)[:, e_ * 8:(e_ + 1) * 8,
                                                                 g_ * 512:(g_ + 1) * 512])
                    dma("gpsimd", sMW[par],
                        w3b[par][:, :].rearrange("p (dc f) -> p dc f", dc=8),
                        d_w3.rearrange("(r p) f -> p r f", p=128)[:, e_ * 8:(e_ + 1) * 8,
                                                                 g_ * 512:(g_ + 1) * 512])
                    mw_tgt[par] = dma("gpsimd", sMW[par],
                                      w2b[par][:, :].rearrange("p (sc d) -> p sc d", sc=4),
                                      d_w2.rearrange("(r p) d -> p r d", p=128)[
                                          :, e_ * 32 + g_ * 4:e_ * 32 + (g_ + 1) * 4, :])
                    wait("tensor", sMW[par], mw_tgt[par])
                    if first:
                        wait("tensor", sZR, zr_all)
                        first = False

                    def h_mms(ts, sc, u):
                        pp = sc % 2
                        wait("tensor", dDVE, hmul_done.get(u - 2, 0))
                        for dc in range(8):
                            op("tensor", "matmul", ph1_[pp][0:128, 0:256],
                               w1b[par][:, dc * 512 + sc * 128:dc * 512 + (sc + 1) * 128],
                               zfm3[:, dc * OWN + ts * 256:dc * OWN + (ts + 1) * 256],
                               start=(dc == 0), stop=(dc == 7))
                        for dc in range(8):
                            op("tensor", "matmul", ph3_[pp][0:128, 0:256],
                               w3b[par][:, dc * 512 + sc * 128:dc * 512 + (sc + 1) * 128],
                               zfm3[:, dc * OWN + ts * 256:dc * OWN + (ts + 1) * 256],
                               start=(dc == 0), stop=(dc == 7),
                               inc=dPE if dc == 7 else None)
                        hv = dPE.v
                        wait("scalar", dPE, hv)
                        wait("scalar", dDVE, hmul_done.get(u - 2, 0))
                        op("scalar", "activation", ht[pp][:, :], ph1_[pp][0:128, 0:256], AF.Silu,
                           inc=dACT)
                        wait("vector", dACT, dACT.v)
                        wait("vector", dPE, w2_done.get(u - 2, 0))
                        op("vector", "tensor_tensor", hbb[pp][:, :], ht[pp][:, :],
                           ph3_[pp][0:128, 0:256], ALU.mult, inc=dDVE)
                        hmul_done[u] = dDVE.v

                    def w2_mms(ts, sc, u, gate_oe):
                        pp = sc % 2
                        wait("tensor", dDVE, hmul_done[u])
                        if gate_oe:
                            wait("tensor", dDVE, gate_oe)
                        for tc2 in range(2):
                            for dh in range(2):
                                op("tensor", "matmul", poe[tc2][0:128, dh * 512:(dh + 1) * 512],
                                   hbb[pp][:, tc2 * 128:(tc2 + 1) * 128],
                                   w2b[par][:, sc * 1024 + dh * 512:sc * 1024 + (dh + 1) * 512],
                                   start=(sc == 0), stop=(sc == 3),
                                   inc=dPE if (tc2 == 1 and dh == 1) else None)
                        w2_done[u] = dPE.v

                    for ts in range(8):
                        u0 = u_lin
                        h_mms(ts, 0, u0 + 0)
                        h_mms(ts, 1, u0 + 1)
                        w2_mms(ts, 0, u0 + 0, oeevac_done)
                        h_mms(ts, 2, u0 + 2)
                        w2_mms(ts, 1, u0 + 1, 0)
                        h_mms(ts, 3, u0 + 3)
                        w2_mms(ts, 2, u0 + 2, 0)
                        w2_mms(ts, 3, u0 + 3, 0)
                        u_lin += 4
                        oe_v = dPE.v
                        wait("vector", dPE, oe_v)
                        for tc2 in range(2):
                            tt = ts * 2 + tc2
                            op("vector", "scalar_tensor_tensor", y2[:, tt * H:(tt + 1) * H],
                               poe[tc2][0:128, :], combine[:, tt * 8 + e_:tt * 8 + e_ + 1],
                               y2[:, tt * H:(tt + 1) * H], op0=ALU.mult, op1=ALU.add,
                               inc=dDVE if tc2 == 1 else None)
                        oeevac_done = dDVE.v
                    groupdone[idx] = dPE.v
                moe_done_dve = dDVE.v

                if _DEBUG == "y2":
                    wait("sync", dDVE, moe_done_dve)
                    dma("sync", sDbg, dbg.rearrange("(tt p) d -> p tt d", p=128),
                        y2[:, :].rearrange("p (tt d) -> p tt d", d=H))
                if _DEBUG == "comb":
                    wait("sync", dDVE, moe_done_dve)
                    dma("sync", sDbg, dbg[:, :], combine[:, :])

                outb = [p3(nc.sbuf_tensor("p4_o0", [128, 1024], F32)),
                        p3(nc.sbuf_tensor("p4_o1", [128, 1024], F32))]
                out_tgt = [0, 0]
                wait("scalar", dDVE, moe_done_dve)
                wait("vector", dDVE, moe_done_dve)
                for tt in range(16):
                    tc = tt % 4
                    osl = tt % 2
                    op("scalar", "activation", sq_trash[:, :], y2[:, tt * H:(tt + 1) * H],
                       AF.Square, accum_out=stat_sq[:, tc:tc + 1], inc=dACT)
                    sq_done = dACT.v
                    drain("vector")
                    op("vector", "reduce_sum", stat_sum[:, tc:tc + 1], y2[:, tt * H:(tt + 1) * H], AX.X)
                    drain("vector")
                    op("vector", "tensor_scalar_mul", mu4[:, tc:tc + 1], stat_sum[:, tc:tc + 1], 1.0 / H)
                    drain("vector")
                    op("vector", "tensor_mul", musq4[:, tc:tc + 1], mu4[:, tc:tc + 1], mu4[:, tc:tc + 1])
                    drain("vector")
                    wait("vector", dACT, sq_done)
                    op("vector", "tensor_scalar", var4[:, tc:tc + 1], stat_sq[:, tc:tc + 1],
                       1.0 / H, None, ALU.mult)
                    drain("vector")
                    op("vector", "tensor_sub", var4[:, tc:tc + 1], var4[:, tc:tc + 1], musq4[:, tc:tc + 1])
                    drain("vector")
                    op("vector", "tensor_scalar", var4[:, tc:tc + 1], var4[:, tc:tc + 1],
                       float(EPS), None, ALU.add, inc=dDVE)
                    wait("scalar", dDVE, dDVE.v)
                    op("scalar", "activation", var4[:, tc:tc + 1], var4[:, tc:tc + 1], AF.Ln)
                    drain("scalar")
                    op("scalar", "activation", rstd4[:, tc:tc + 1], var4[:, tc:tc + 1],
                       AF.Exp, scale=-0.5, inc=dACT)
                    wait("vector", dACT, dACT.v)
                    op("vector", "scalar_tensor_tensor", nmr4[:, tc:tc + 1], mu4[:, tc:tc + 1],
                       -1.0, rstd4[:, tc:tc + 1], op0=ALU.mult, op1=ALU.mult, inc=dDVE)
                    wait("scalar", dDVE, dDVE.v)
                    wait("scalar", sOut, out_tgt[osl])
                    op("scalar", "activation", outb[osl][:, :], y2[:, tt * H:(tt + 1) * H],
                       AF.Identity, scale=rstd4[:, tc:tc + 1], bias=nmr4[:, tc:tc + 1], inc=dACT)
                    wait("sync", dACT, dACT.v)
                    out_tgt[osl] = dma("sync", sOut, d_out[tt * 128:(tt + 1) * 128, :],
                                       outb[osl][:, :])

                wait("sync", sOut, sOut.v)
                if _DEBUG is not None and sDbg.v:
                    wait("sync", sDbg, sDbg.v)

        for s_ in sems:
            wait("sync", s_, s_.v)

        with contextlib.ExitStack() as semstack:
            for s in sems:
                s.h = semstack.enter_context(nc.semaphore(s.name))
            with nc.Block() as block:
                @block.sync
                def _(eng):
                    for f in prog["sync"]:
                        f(eng)

                @block.scalar
                def _(eng):
                    for f in prog["scalar"]:
                        f(eng)

                @block.vector
                def _(eng):
                    for f in prog["vector"]:
                        f(eng)

                @block.tensor
                def _(eng):
                    for f in prog["tensor"]:
                        f(eng)

                @block.gpsimd
                def _(eng):
                    for f in prog["gpsimd"]:
                        f(eng)
    return nc



def _rope_tables():
    inv = np.array(_INV_FREQ_BITS, dtype=np.uint32).view(np.float32)
    pos = np.arange(T, dtype=np.float32)
    args = (pos[:, None] * inv[None, :]).astype(np.float32)
    return np.sin(args).astype(np.float32), np.cos(args).astype(np.float32)


def _beta(w, d):
    r = d + 511
    r = np.where(r < 0, r + 1023, r)
    return w[np.clip(r, 0, 1022)]


def _sh2ext(w, h):
    p = np.arange(128)[:, None]
    m = np.arange(8064)[None, :]
    da = p - m + 1920
    db = p - m + 1920 + 6144 - 4096 * h
    return np.where(m < 3968, _beta(w, da), _beta(w, db)).astype(np.float32)


def kernel(L0_attn_x, L0_attn_ln_w, L0_attn_ln_b, L0_attn_uv_w, L0_attn_uv_b,
           L0_attn_gamma, L0_attn_beta, L0_attn_w, L0_attn_o_w, L0_attn_o_b,
           L0_ffn_gate_w, L0_ffn_gate_b, L0_ffn_w1, L0_ffn_w3, L0_ffn_w2):
    import ml_dtypes
    x = np.ascontiguousarray(np.asarray(L0_attn_x, dtype=np.float32))
    uv_w = np.ascontiguousarray(np.asarray(L0_attn_uv_w, dtype=np.float32))
    uv_b = np.ascontiguousarray(np.asarray(L0_attn_uv_b, dtype=np.float32))
    ln_w = np.ascontiguousarray(np.asarray(L0_attn_ln_w, dtype=np.float32)).reshape(H, 1)
    ln_b = np.ascontiguousarray(np.asarray(L0_attn_ln_b, dtype=np.float32)).reshape(H, 1)
    gamma = np.asarray(L0_attn_gamma, dtype=np.float32)
    beta = np.asarray(L0_attn_beta, dtype=np.float32)
    w = np.ascontiguousarray(np.asarray(L0_attn_w, dtype=np.float32))
    o_w = np.ascontiguousarray(np.asarray(L0_attn_o_w, dtype=np.float32))
    o_b = np.ascontiguousarray(np.asarray(L0_attn_o_b, dtype=np.float32)).reshape(H, 1)
    gate_w = np.ascontiguousarray(np.asarray(L0_ffn_gate_w, dtype=np.float32))
    gate_b = np.ascontiguousarray(np.asarray(L0_ffn_gate_b, dtype=np.float32)).reshape(1, 8)
    w1 = np.ascontiguousarray(np.asarray(L0_ffn_w1, dtype=np.float32)).reshape(8 * H, FFN)
    w3 = np.ascontiguousarray(np.asarray(L0_ffn_w3, dtype=np.float32)).reshape(8 * H, FFN)
    w2 = np.ascontiguousarray(np.asarray(L0_ffn_w2, dtype=np.float32)).reshape(8 * FFN, H)

    sin_tm, cos_tm = _rope_tables()
    shared = dict(
        uv_w=uv_w, uv_b_row=uv_b.reshape(1, UVC),
        uv_b_col=np.ascontiguousarray(uv_b.reshape(UVC, 1)),
        ln_w=ln_w, ln_b=ln_b,
        gamma0=np.ascontiguousarray(gamma[0].reshape(S, 1)),
        gamma1=np.ascontiguousarray(gamma[1].reshape(S, 1)),
        beta0=np.ascontiguousarray(beta[0].reshape(S, 1)),
        beta1=np.ascontiguousarray(beta[1].reshape(S, 1)),
        o_w=o_w, o_b=o_b, gate_w=gate_w, gate_b=gate_b,
        w1=w1, w3=w3, w2=w2,
        idn32=np.eye(128, dtype=np.float32),
        idn16=np.eye(128).astype(ml_dtypes.bfloat16),
        ones_row=np.ones((1, 128), np.float32),
        iota8=np.tile(np.arange(8, dtype=np.float32)[None, :], (128, 1)),
        iota8b=np.tile(np.arange(8, dtype=np.float32)[None, :], (128, 1)) + BIG,
    )
    sh2 = [_sh2ext(w, hh) for hh in (0, 1)]

    in_maps = []
    for c in range(8):
        b, hh = c // 2, c % 2
        own = x[b, 2048 * hh:2048 * hh + 2048]
        oth = x[b, 2048 * (1 - hh):2048 * (1 - hh) + 2048]
        x_p = np.ascontiguousarray(np.concatenate([own, oth], axis=0))
        perm = np.concatenate([np.arange(2048 * hh, 2048 * hh + 2048),
                               np.arange(2048 * (1 - hh), 2048 * (1 - hh) + 2048)])
        m = dict(shared)
        m["x"] = x_p
        m["sh2ext"] = sh2[hh]
        m["sin_tm"] = np.ascontiguousarray(sin_tm[perm])
        m["cos_tm"] = np.ascontiguousarray(cos_tm[perm])
        in_maps.append(m)

    nc = build_program()
    res = run_bass_kernel_spmd(nc, in_maps, core_ids=list(range(8)))

    out = np.empty((4, 4096, 1024), np.float32)
    for c in range(8):
        b, hh = c // 2, c % 2
        out[b, 2048 * hh:2048 * hh + 2048] = res.results[c]["out"]
    if _DEBUG is not None:
        kernel._dbg = [r.get("dbg") for r in res.results]
    return out


# version 03
# speedup vs baseline: 1.0544x; 1.0544x over previous
import contextlib
import numpy as np

import concourse.bass as bass
import concourse.mybir as mybir
from concourse.bass_utils import run_bass_kernel_spmd

F32 = mybir.dt.float32
BF16 = mybir.dt.bfloat16
AF = mybir.ActivationFunctionType
ALU = mybir.AluOpType
AX = mybir.AxisListType

_INV_FREQ_BITS = [
    1065353216, 1066651621, 1068150997, 1069882448, 1071881897, 1073966324,
    1075299478, 1076838980, 1078616770, 1080669729, 1082585441, 1083954273,
    1085534976, 1087360344, 1089468245, 1091210726, 1092616192, 1094239199,
    1096113418, 1098277732, 1099842346, 1101285425, 1102951868, 1104876246,
    1107098482, 1108480469, 1109962169, 1111673209, 1113649088, 1115807831,
    1117125269, 1118646624, 1120403456, 1122432214, 1124424230, 1125776926,
    1127338996, 1129142846, 1131225899, 1133046725, 1134435623, 1136039498,
    1137891623, 1140030424, 1141675480, 1143101549, 1144748347, 1146650039,
    1148846080, 1150310662, 1151774896, 1153465766, 1155418353, 1157650535,
    1158952443, 1160455863, 1162191985, 1164196828, 1166264248, 1167600999,
    1169144654, 1170927240, 1172985737, 1174883987,
]

T = 4096
OWN = 2048
H = 1024
E = 2048
S = 128
UVC = 4224
FFN = 4096
NSUP = 8
QS = 4
RSQ = float(1.0 / np.sqrt(128.0))
EPS = 1e-5
BIG = 65536.0

_DEBUG = None
_LIMIT = 9
_P2S = 9
_QSN = 4
_NOZFMST = False


class SemCounter:
    def __init__(self, name):
        self.name = name
        self.h = None
        self.v = 0

    def inc(self, n):
        self.v += n
        return self.v


def build_program():
    nc = bass.Bass("TRN2", target_bir_lowering=False)

    d_x = nc.dram_tensor("x", [T, H], F32, kind="ExternalInput").ap()
    d_uvw = nc.dram_tensor("uv_w", [UVC, H], F32, kind="ExternalInput").ap()
    d_uvb_r = nc.dram_tensor("uv_b_row", [1, UVC], F32, kind="ExternalInput").ap()
    d_uvb_c = nc.dram_tensor("uv_b_col", [UVC, 1], F32, kind="ExternalInput").ap()
    d_lnw = nc.dram_tensor("ln_w", [H, 1], F32, kind="ExternalInput").ap()
    d_lnb = nc.dram_tensor("ln_b", [H, 1], F32, kind="ExternalInput").ap()
    d_g0 = nc.dram_tensor("gamma0", [S, 1], F32, kind="ExternalInput").ap()
    d_g1 = nc.dram_tensor("gamma1", [S, 1], F32, kind="ExternalInput").ap()
    d_b0 = nc.dram_tensor("beta0", [S, 1], F32, kind="ExternalInput").ap()
    d_b1 = nc.dram_tensor("beta1", [S, 1], F32, kind="ExternalInput").ap()
    d_ow = nc.dram_tensor("o_w", [H, E], F32, kind="ExternalInput").ap()
    d_ob = nc.dram_tensor("o_b", [H, 1], F32, kind="ExternalInput").ap()
    d_gw = nc.dram_tensor("gate_w", [H, 8], F32, kind="ExternalInput").ap()
    d_gb = nc.dram_tensor("gate_b", [1, 8], F32, kind="ExternalInput").ap()
    d_w1 = nc.dram_tensor("w1", [8 * H, FFN], F32, kind="ExternalInput").ap()
    d_w3 = nc.dram_tensor("w3", [8 * H, FFN], F32, kind="ExternalInput").ap()
    d_w2 = nc.dram_tensor("w2", [8 * FFN, H], F32, kind="ExternalInput").ap()
    d_sh2 = nc.dram_tensor("sh2ext", [128, 8064], F32, kind="ExternalInput").ap()
    d_sin = nc.dram_tensor("sin_tm", [T, 64], F32, kind="ExternalInput").ap()
    d_cos = nc.dram_tensor("cos_tm", [T, 64], F32, kind="ExternalInput").ap()
    d_id32 = nc.dram_tensor("idn32", [128, 128], F32, kind="ExternalInput").ap()
    d_id16 = nc.dram_tensor("idn16", [128, 128], BF16, kind="ExternalInput").ap()
    d_ones = nc.dram_tensor("ones_row", [1, 128], F32, kind="ExternalInput").ap()
    d_iota = nc.dram_tensor("iota8", [128, 8], F32, kind="ExternalInput").ap()
    d_iotab = nc.dram_tensor("iota8b", [128, 8], F32, kind="ExternalInput").ap()

    d_out = nc.dram_tensor("out", [OWN, H], F32, kind="ExternalOutput").ap()

    d_vsc = nc.dram_tensor("v_sc", [T, E], BF16, kind="Internal").ap()
    d_xnfm = nc.dram_tensor("xnfm_sc", [NSUP * 128, 4096], BF16, kind="Internal").ap()
    d_uct = nc.dram_tensor("ucolsT_sc", [H, E], BF16, kind="Internal").ap()
    d_owt = nc.dram_tensor("owT_sc", [E, H], BF16, kind="Internal").ap()
    d_ztm = nc.dram_tensor("ztm_sc", [OWN, H], F32, kind="Internal").ap()
    d_zfm = nc.dram_tensor("zfm_sc", [128, 8 * OWN], BF16, kind="Internal").ap()

    dbg = None
    if _DEBUG == "v":
        dbg = nc.dram_tensor("dbg", [T, E], F32, kind="ExternalOutput").ap()
    elif _DEBUG == "xnfm":
        dbg = nc.dram_tensor("dbg", [QS * 128, 4096], F32, kind="ExternalOutput").ap()
    elif _DEBUG == "z":
        dbg = nc.dram_tensor("dbg", [OWN, H], F32, kind="ExternalOutput").ap()
    elif _DEBUG == "zfm":
        dbg = nc.dram_tensor("dbg", [128, 8 * OWN], F32, kind="ExternalOutput").ap()
    elif _DEBUG == "uct":
        dbg = nc.dram_tensor("dbg", [H, E], F32, kind="ExternalOutput").ap()
    elif _DEBUG == "owt":
        dbg = nc.dram_tensor("dbg", [E, H], F32, kind="ExternalOutput").ap()
    elif _DEBUG == "y2":
        dbg = nc.dram_tensor("dbg", [OWN, H], F32, kind="ExternalOutput").ap()
    elif _DEBUG == "comb":
        dbg = nc.dram_tensor("dbg", [128, 128], F32, kind="ExternalOutput").ap()

    prog = {"sync": [], "scalar": [], "vector": [], "tensor": [], "gpsimd": []}

    def emit(eng, fn):
        prog[eng].append(fn)

    sems = []

    def new_sem(name):
        s = SemCounter(name)
        sems.append(s)
        return s

    dPE = new_sem("dPE")
    dACT = new_sem("dACT")
    dDVE = new_sem("dDVE")
    sSm = new_sem("sSm")
    sW0 = [new_sem("sW0a"), new_sem("sW0b")]
    sSt0 = new_sem("sSt0")
    sX = [new_sem("sXa"), new_sem("sXb")]
    sVs = [new_sem("sVsa"), new_sem("sVsb")]
    sXnW = new_sem("sXnW")
    sXf = new_sem("sXf")
    sX2 = new_sem("sX2")
    sUc = [new_sem("sUc0"), new_sem("sUc1")]
    sOw = [new_sem("sOw0"), new_sem("sOw1")]
    sSh = [new_sem("sSh0"), new_sem("sSh1")]
    sVq = [new_sem("sVq0"), new_sem("sVq1"), new_sem("sVq2")]
    sZW = new_sem("sZW")
    sZR = new_sem("sZR")
    sMW = [new_sem("sMW0"), new_sem("sMW1")]
    sOut = new_sem("sOut")
    sDbg = new_sem("sDbg")
    sUvb = new_sem("sUvb")
    sSc = new_sem("sSc")

    def wait(eng, sem, val):
        if val and val > 0:
            emit(eng, lambda e, s=sem, v=val: e.wait_ge(s.h, v))

    def drain(eng):
        emit(eng, lambda e: e.drain())

    def dma(eng, sem, out, in_, **kw):
        tgt = sem.inc(16)
        emit(eng, lambda e, o=out, i=in_, s=sem, kw=kw: e.dma_start(out=o, in_=i, **kw).then_inc(s.h, 16))
        return tgt

    def op(eng, method, *args, inc=None, **kw):
        if inc is not None:
            inc.inc(1)

        def f(e, m=method, a=args, kw=kw, s=inc):
            r = getattr(e, m)(*a, **kw)
            if s is not None:
                r.then_inc(s.h, 1)
        emit(eng, f)
        return None if inc is None else inc.v

    with contextlib.ExitStack() as outer:
        octx = outer.enter_context

        ones_row = octx(nc.sbuf_tensor("c_ones", [1, 128], F32))
        idn32 = octx(nc.sbuf_tensor("c_id32", [128, 128], F32))
        idn16 = octx(nc.sbuf_tensor("c_id16", [128, 128], BF16))
        iota8 = octx(nc.sbuf_tensor("c_iota", [128, 8], F32))
        iota8b = octx(nc.sbuf_tensor("c_iotab", [128, 8], F32))
        lnw_c = octx(nc.sbuf_tensor("c_lnw", [128, 8], F32))
        lnb_c = octx(nc.sbuf_tensor("c_lnb", [128, 8], F32))
        ob_c = octx(nc.sbuf_tensor("c_ob", [128, 8], F32))
        uvbu_c = octx(nc.sbuf_tensor("c_uvbu", [128, 16], F32))
        uvbs_c = octx(nc.sbuf_tensor("c_uvbs", [128, 1], F32))
        g0c = octx(nc.sbuf_tensor("c_g0", [128, 1], F32))
        b0c = octx(nc.sbuf_tensor("c_b0", [128, 1], F32))
        g1c = octx(nc.sbuf_tensor("c_g1", [128, 1], F32))
        b1c = octx(nc.sbuf_tensor("c_b1", [128, 1], F32))
        g0s = octx(nc.sbuf_tensor("c_g0s", [128, 1], F32))
        b0s = octx(nc.sbuf_tensor("c_b0s", [128, 1], F32))
        gw_sb = octx(nc.sbuf_tensor("c_gw", [128, 64], F32))
        gb_row = octx(nc.sbuf_tensor("c_gb", [1, 8], F32))
        k_fm = octx(nc.sbuf_tensor("k_fm", [128, T], BF16))
        q_fm = octx(nc.sbuf_tensor("q_fm", [128, OWN], BF16))
        combine = octx(nc.sbuf_tensor("combine", [128, 16 * 8], F32))
        stat_sum = octx(nc.sbuf_tensor("st_sum", [128, 4], F32))
        stat_sq = octx(nc.sbuf_tensor("st_sq", [128, 4], F32))
        mu4 = octx(nc.sbuf_tensor("st_mu", [128, 4], F32))
        musq4 = octx(nc.sbuf_tensor("st_musq", [128, 4], F32))
        var4 = octx(nc.sbuf_tensor("st_var", [128, 4], F32))
        rstd4 = octx(nc.sbuf_tensor("st_rstd", [128, 4], F32))
        nmr4 = octx(nc.sbuf_tensor("st_nmr", [128, 4], F32))
        sq_trash = octx(nc.sbuf_tensor("sq_trash", [128, 1024], F32))

        dma("sync", sSm, ones_row[:, :], d_ones[:, :])
        dma("sync", sSm, idn32[:, :], d_id32[:, :])
        dma("sync", sSm, idn16[:, :], d_id16[:, :])
        dma("sync", sSm, iota8[:, :], d_iota[:, :])
        dma("sync", sSm, iota8b[:, :], d_iotab[:, :])
        for fc in range(8):
            dma("sync", sSm, lnw_c[:, fc:fc + 1], d_lnw[fc * 128:(fc + 1) * 128, :])
            dma("sync", sSm, lnb_c[:, fc:fc + 1], d_lnb[fc * 128:(fc + 1) * 128, :])
            dma("sync", sSm, ob_c[:, fc:fc + 1], d_ob[fc * 128:(fc + 1) * 128, :])
        for ec in range(16):
            dma("sync", sSm, uvbu_c[:, ec:ec + 1], d_uvb_c[ec * 128:(ec + 1) * 128, :])
        dma("sync", sSm, uvbs_c[:, :], d_uvb_c[4096:4224, :])
        dma("sync", sSm, g0c[:, :], d_g0[:, :])
        dma("sync", sSm, b0c[:, :], d_b0[:, :])
        dma("sync", sSm, g1c[:, :], d_g1[:, :])
        dma("sync", sSm, b1c[:, :], d_b1[:, :])
        dma("sync", sSm, gw_sb[:, :].rearrange("p (fc e) -> p fc e", fc=8),
            d_gw.rearrange("(fc p) e -> p fc e", p=128))
        dma("sync", sSm, gb_row[:, :], d_gb[:, :])
        sm_consts = sSm.v

        wait("vector", sSm, sm_consts)
        op("vector", "tensor_scalar_mul", g0s[:, :], g0c[:, :], RSQ)
        op("vector", "tensor_scalar_mul", b0s[:, :], b0c[:, :], RSQ, inc=dDVE)
        drain("vector")

        scopeA = contextlib.ExitStack()
        uvwT_vs = scopeA.enter_context(nc.sbuf_tensor("uvwT_vs", [128, 8 * 2176], BF16))
        uvb_row = scopeA.enter_context(nc.sbuf_tensor("c_uvbr", [1, UVC], F32))
        dma("sync", sUvb, uvb_row[:, :], d_uvb_r[:, :])

        with contextlib.ExitStack() as ph0:
            p0 = ph0.enter_context
            ldbuf = [p0(nc.sbuf_tensor("p0_ld0", [128, 2048], F32)),
                     p0(nc.sbuf_tensor("p0_ld1", [128, 2048], F32))]
            u_stage = p0(nc.sbuf_tensor("p0_ustage", [128, 8 * E], BF16))
            o_stage = p0(nc.sbuf_tensor("p0_ostage", [128, 16 * H], BF16))
            pt = [p0(nc.psum_tensor("p0_pt0", [128, 1024], F32)),
                  p0(nc.psum_tensor("p0_pt1", [128, 1024], F32))]
            ld_tgt = [0, 0]
            evac_done = [0, 0]

            wait("tensor", sSm, sm_consts)
            for cc in range(33):
                sl = cc % 2
                wait("sync", dACT, evac_done[sl])
                ld_tgt[sl] = dma("sync", sW0[sl], ldbuf[sl][:, 0:1024],
                                 d_uvw[cc * 128:(cc + 1) * 128, :])
                wait("tensor", sW0[sl], ld_tgt[sl])
                wait("tensor", dACT, evac_done[sl])
                for fc in range(8):
                    op("tensor", "transpose", pt[sl][0:128, fc * 128:(fc + 1) * 128],
                       ldbuf[sl][:, fc * 128:(fc + 1) * 128], idn32[:, :],
                       inc=dPE if fc == 7 else None)
                wait("scalar", dPE, dPE.v)
                if cc < 16:
                    out_ap = u_stage[:, :].rearrange("p (fc e) -> p fc e", fc=8)[:, :, cc * 128:(cc + 1) * 128]
                elif cc < 32:
                    out_ap = uvwT_vs[:, :].rearrange("p (fc c) -> p fc c", fc=8)[:, :, (cc - 16) * 128:(cc - 15) * 128]
                else:
                    out_ap = uvwT_vs[:, :].rearrange("p (fc c) -> p fc c", fc=8)[:, :, 2048:2176]
                in_ap = pt[sl][0:128, 0:1024].rearrange("p (fc c) -> p fc c", fc=8)
                op("scalar", "activation", out_ap, in_ap, AF.Copy, inc=dACT)
                evac_done[sl] = dACT.v
            wait("sync", dACT, dACT.v)
            dma("sync", sSt0, d_uct.rearrange("(fc p) e -> p fc e", p=128),
                u_stage[:, :].rearrange("p (fc e) -> p fc e", fc=8))

            for hc in range(8):
                sl = hc % 2
                wait("sync", dACT, evac_done[sl])
                ld_tgt[sl] = dma("sync", sW0[sl], ldbuf[sl][:, 0:2048],
                                 d_ow[hc * 128:(hc + 1) * 128, :])
                wait("tensor", sW0[sl], ld_tgt[sl])
                for half in range(2):
                    wait("tensor", dACT, evac_done[sl])
                    for j in range(8):
                        ec = half * 8 + j
                        op("tensor", "transpose", pt[sl][0:128, j * 128:(j + 1) * 128],
                           ldbuf[sl][:, ec * 128:(ec + 1) * 128], idn32[:, :],
                           inc=dPE if j == 7 else None)
                    wait("scalar", dPE, dPE.v)
                    out_ap = o_stage[:, :].rearrange("p (ec h) -> p ec h", ec=16)[
                        :, half * 8:(half + 1) * 8, hc * 128:(hc + 1) * 128]
                    in_ap = pt[sl][0:128, 0:1024].rearrange("p (j c) -> p j c", j=8)
                    op("scalar", "activation", out_ap, in_ap, AF.Copy, inc=dACT)
                    evac_done[sl] = dACT.v
            wait("sync", dACT, dACT.v)
            dma("sync", sSt0, d_owt.rearrange("(ec p) h -> p ec h", p=128),
                o_stage[:, :].rearrange("p (ec h) -> p ec h", ec=16))
            st0_all = sSt0.v
            ph0_pe, ph0_act = dPE.v, dACT.v

            if _DEBUG == "uct":
                wait("gpsimd", sSt0, st0_all)
                dma("gpsimd", sDbg, dbg[:, :], d_uct[:, :])
            if _DEBUG == "owt":
                wait("gpsimd", sSt0, st0_all)
                dma("gpsimd", sDbg, dbg[:, :], d_owt[:, :])

        if _LIMIT >= 1:
            with contextlib.ExitStack() as ph1:
                p1 = ph1.enter_context
                xsup = [p1(nc.sbuf_tensor("p1_x0", [128, 4096], F32)),
                        p1(nc.sbuf_tensor("p1_x1", [128, 4096], F32))]
                xn2 = p1(nc.sbuf_tensor("p1_xn2", [128, 4096], F32))
                xnfm = [p1(nc.sbuf_tensor("p1_xnfm0", [128, 4096], BF16)),
                        p1(nc.sbuf_tensor("p1_xnfm1", [128, 4096], BF16))]
                xnrb = [p1(nc.sbuf_tensor("p1_xnr0", [128, 4096], BF16)),
                        p1(nc.sbuf_tensor("p1_xnr1", [128, 4096], BF16))]
                vstage = [p1(nc.sbuf_tensor("p1_vs0", [128, 2048], BF16)),
                          p1(nc.sbuf_tensor("p1_vs1", [128, 2048], BF16))]
                basek = p1(nc.sbuf_tensor("p1_basek", [128, 512], F32))
                kb = p1(nc.sbuf_tensor("p1_kb", [128, 512], F32))
                kb_tm = p1(nc.sbuf_tensor("p1_kbtm", [128, 512], F32))
                krp = p1(nc.sbuf_tensor("p1_krp", [128, 512], BF16))
                m1 = p1(nc.sbuf_tensor("p1_m1", [128, 64], F32))
                m2 = p1(nc.sbuf_tensor("p1_m2", [128, 64], F32))
                m3 = p1(nc.sbuf_tensor("p1_m3", [128, 64], F32))
                m4 = p1(nc.sbuf_tensor("p1_m4", [128, 64], F32))
                sin_tm = p1(nc.sbuf_tensor("p1_sin", [128, 32 * 64], F32))
                cos_tm = p1(nc.sbuf_tensor("p1_cos", [128, 32 * 64], F32))
                ptr = [p1(nc.psum_tensor("p1_ptr0", [128, 512], F32)),
                       p1(nc.psum_tensor("p1_ptr1", [128, 512], F32))]
                pvs = [p1(nc.psum_tensor("p1_pv0", [128, 512], F32)),
                       p1(nc.psum_tensor("p1_pv1", [128, 512], F32))]
                pkq = [p1(nc.psum_tensor("p1_pk0", [128, 512], F32)),
                       p1(nc.psum_tensor("p1_pk1", [128, 512], F32))]
                pbf = p1(nc.psum_tensor("p1_pbf", [128, 1024], BF16))

                wait("sync", sSt0, st0_all)
                wait("sync", dPE, ph0_pe)
                dma("sync", sSc, sin_tm[:, :].rearrange("p (t f) -> p t f", f=64),
                    d_sin.rearrange("(t p) f -> p t f", p=128))
                dma("sync", sSc, cos_tm[:, :].rearrange("p (t f) -> p t f", f=64),
                    d_cos.rearrange("(t p) f -> p t f", p=128))
                wait("vector", sSc, sSc.v)
                wait("vector", sSm, sm_consts)
                wait("tensor", sSm, sm_consts)
                wait("tensor", sUvb, sUvb.v)

                x_tgt = [0, 0]
                x_lastread = [0, 0]
                xnst_tgt = [0, 0]
                for s in range(NSUP):
                    sl = s % 2
                    wait("sync", dDVE, x_lastread[sl])
                    x_tgt[sl] = dma("sync", sX[sl],
                                    xsup[sl][:, :].rearrange("p (tc f) -> p tc f", f=H),
                                    d_x.rearrange("(tc p) f -> p tc f", p=128)[:, s * 4:(s + 1) * 4, :])
                    wait("scalar", sX[sl], x_tgt[sl])
                    for tc in range(4):
                        if tc:
                            drain("scalar")
                        op("scalar", "activation", sq_trash[:, :], xsup[sl][:, tc * H:(tc + 1) * H],
                           AF.Square, accum_out=stat_sq[:, tc:tc + 1],
                           inc=dACT if tc == 3 else None)
                    sq_done = dACT.v
                    wait("vector", sX[sl], x_tgt[sl])
                    for tc in range(4):
                        op("vector", "reduce_sum", stat_sum[:, tc:tc + 1],
                           xsup[sl][:, tc * H:(tc + 1) * H], AX.X)
                    drain("vector")
                    op("vector", "tensor_scalar_mul", mu4[:, :], stat_sum[:, :], 1.0 / H)
                    drain("vector")
                    op("vector", "tensor_mul", musq4[:, :], mu4[:, :], mu4[:, :])
                    drain("vector")
                    wait("vector", dACT, sq_done)
                    op("vector", "tensor_scalar", var4[:, :], stat_sq[:, :], 1.0 / H, None, ALU.mult)
                    drain("vector")
                    op("vector", "tensor_sub", var4[:, :], var4[:, :], musq4[:, :])
                    drain("vector")
                    op("vector", "tensor_scalar", var4[:, :], var4[:, :], float(EPS), None, ALU.add,
                       inc=dDVE)
                    wait("scalar", dDVE, dDVE.v)
                    op("scalar", "activation", var4[:, :], var4[:, :], AF.Ln)
                    drain("scalar")
                    op("scalar", "activation", rstd4[:, :], var4[:, :], AF.Exp, scale=-0.5,
                       inc=dACT)
                    wait("vector", dACT, dACT.v)
                    op("vector", "scalar_tensor_tensor", nmr4[:, :], mu4[:, :], -1.0, rstd4[:, :],
                       op0=ALU.mult, op1=ALU.mult)
                    drain("vector")
                    for tc in range(4):
                        op("vector", "tensor_scalar", xn2[:, tc * H:(tc + 1) * H],
                           xsup[sl][:, tc * H:(tc + 1) * H],
                           rstd4[:, tc:tc + 1], nmr4[:, tc:tc + 1], ALU.mult, ALU.add,
                           inc=dDVE if tc == 3 else None)
                    xn2_done = dDVE.v
                    x_lastread[sl] = xn2_done

                    ln_aff = [0] * 8
                    wait("tensor", dDVE, xn2_done)
                    wait("scalar", sXnW, xnst_tgt[sl])
                    for fc in range(8):
                        tsl = fc % 2
                        if fc >= 2:
                            wait("tensor", dACT, ln_aff[fc - 2])
                        for tc in range(4):
                            op("tensor", "transpose", ptr[tsl][0:128, tc * 128:(tc + 1) * 128],
                               xn2[:, tc * H + fc * 128:tc * H + (fc + 1) * 128], idn32[:, :],
                               inc=dPE if tc == 3 else None)
                        wait("scalar", dPE, dPE.v)
                        op("scalar", "activation", xnfm[sl][:, fc * 512:(fc + 1) * 512],
                           ptr[tsl][0:128, 0:512], AF.Identity,
                           scale=lnw_c[:, fc:fc + 1], bias=lnb_c[:, fc:fc + 1], inc=dACT)
                        ln_aff[fc] = dACT.v
                    wait("sync", dACT, dACT.v)
                    xnst_tgt[sl] = dma("sync", sXnW, d_xnfm[s * 128:(s + 1) * 128, :],
                                       xnfm[sl][:, :])
                xnw_all = sXnW.v

                vst_tgt = [0, 0]
                vev_done = {}
                vev_lin = 0
                kq_evac = [0, 0]
                basek_lastdve = 0
                kbtm_lastdve = 0
                krp_lastpe = 0
                pbf_lastread = 0
                xnrb_tgt = [0, 0]
                xnrb_lastpe = [0, 0]
                for s in range(NSUP):
                    sl = s % 2
                    wait("sync", sXnW, xnw_all)
                    wait("sync", dPE, xnrb_lastpe[sl])
                    xnrb_tgt[sl] = dma("sync", sXf, xnrb[sl][:, :],
                                       d_xnfm[s * 128:(s + 1) * 128, :])
                    xnf = xnrb[sl]
                    wait("tensor", sXf, xnrb_tgt[sl])
                    for tc in range(4):
                        vsl = tc % 2
                        for eg in range(4):
                            psl = vev_lin % 2
                            wait("tensor", dACT, vev_done.get(vev_lin - 2, 0))
                            op("tensor", "matmul", pvs[psl][0:128, :],
                               ones_row[0:1, 0:128], uvb_row[0:1, E + eg * 512:E + (eg + 1) * 512],
                               start=True, stop=False)
                            for fc in range(8):
                                op("tensor", "matmul", pvs[psl][0:128, :],
                                   xnf[:, fc * 512 + tc * 128:fc * 512 + (tc + 1) * 128],
                                   uvwT_vs[:, fc * 2176 + eg * 512:fc * 2176 + (eg + 1) * 512],
                                   start=False, stop=(fc == 7),
                                   inc=dPE if fc == 7 else None)
                            wait("scalar", dPE, dPE.v)
                            if eg == 0:
                                wait("scalar", sVs[vsl], vst_tgt[vsl])
                            op("scalar", "activation", vstage[vsl][:, eg * 512:(eg + 1) * 512],
                               pvs[psl][0:128, :], AF.Silu, inc=dACT)
                            vev_done[vev_lin] = dACT.v
                            vev_lin += 1
                        wait("sync", dACT, dACT.v)
                        tcg = s * 4 + tc
                        vst_tgt[vsl] = dma("sync", sVs[vsl],
                                           d_vsc[tcg * 128:(tcg + 1) * 128, :], vstage[vsl][:, :])

                    for which in (["k", "q"] if s < QS else ["k"]):
                        ksl = 0 if which == "k" else 1
                        wait("tensor", dACT, kq_evac[ksl])
                        for fc in range(8):
                            op("tensor", "matmul", pkq[ksl][0:128, :],
                               uvwT_vs[:, fc * 2176 + 2048:fc * 2176 + 2176],
                               xnf[:, fc * 512:(fc + 1) * 512],
                               start=(fc == 0), stop=(fc == 7),
                               inc=dPE if fc == 7 else None)
                        wait("scalar", dPE, dPE.v)
                        wait("scalar", dDVE, basek_lastdve)
                        op("scalar", "activation", basek[:, :], pkq[ksl][0:128, :], AF.Silu,
                           bias=uvbs_c[:, :], inc=dACT)
                        kq_evac[ksl] = dACT.v
                        wait("vector", dACT, dACT.v)
                        gc, bc = (g1c, b1c) if which == "k" else (g0s, b0s)
                        op("vector", "tensor_scalar", kb[:, :], basek[:, :],
                           gc[:, :], bc[:, :], ALU.mult, ALU.add, inc=dDVE)
                        kb_done = dDVE.v
                        basek_lastdve = kb_done
                        wait("tensor", dDVE, kb_done)
                        for tc in range(4):
                            op("tensor", "transpose", pkq[ksl][0:128, tc * 128:(tc + 1) * 128],
                               kb[:, tc * 128:(tc + 1) * 128], idn32[:, :],
                               inc=dPE if tc == 3 else None)
                        wait("scalar", dPE, dPE.v)
                        wait("scalar", dDVE, kbtm_lastdve)
                        op("scalar", "activation", kb_tm[:, :], pkq[ksl][0:128, :], AF.Copy,
                           inc=dACT)
                        kq_evac[ksl] = dACT.v
                        wait("vector", dACT, dACT.v)
                        wait("vector", dPE, krp_lastpe)
                        for tc in range(4):
                            t0 = tc * 128
                            so = (s * 4 + tc) * 64
                            if tc:
                                drain("vector")
                            op("vector", "tensor_mul", m1[:, :], kb_tm[:, t0:t0 + 64], cos_tm[:, so:so + 64])
                            op("vector", "tensor_mul", m2[:, :], kb_tm[:, t0 + 64:t0 + 128], sin_tm[:, so:so + 64])
                            op("vector", "tensor_mul", m3[:, :], kb_tm[:, t0 + 64:t0 + 128], cos_tm[:, so:so + 64])
                            op("vector", "tensor_mul", m4[:, :], kb_tm[:, t0:t0 + 64], sin_tm[:, so:so + 64])
                            drain("vector")
                            op("vector", "tensor_sub", krp[:, t0:t0 + 64], m1[:, :], m2[:, :])
                            op("vector", "tensor_add", krp[:, t0 + 64:t0 + 128], m3[:, :], m4[:, :],
                               inc=dDVE if tc == 3 else None)
                        krp_done = dDVE.v
                        kbtm_lastdve = krp_done
                        wait("tensor", dDVE, krp_done)
                        wait("tensor", dDVE, pbf_lastread)
                        for tc in range(4):
                            op("tensor", "transpose", pbf[0:128, tc * 128:(tc + 1) * 128],
                               krp[:, tc * 128:(tc + 1) * 128], idn16[:, :],
                               inc=dPE if tc == 3 else None)
                        krp_lastpe = dPE.v
                        wait("vector", dPE, dPE.v)
                        dst = k_fm[:, s * 512:(s + 1) * 512] if which == "k" else q_fm[:, s * 512:(s + 1) * 512]
                        op("vector", "tensor_copy", dst, pbf[0:128, 0:512], inc=dDVE)
                        pbf_lastread = dDVE.v
                    xnrb_lastpe[sl] = dPE.v

                ph1_pe, ph1_dve, ph1_act = dPE.v, dDVE.v, dACT.v
                vst_all = [sVs[0].v, sVs[1].v]

                if _DEBUG == "xnfm":
                    wait("gpsimd", sXnW, xnw_all)
                    dma("gpsimd", sDbg, dbg[:, :], d_xnfm[0:QS * 128, :])
                if _DEBUG == "v":
                    wait("gpsimd", sVs[0], vst_all[0])
                    wait("gpsimd", sVs[1], vst_all[1])
                    dma("gpsimd", sDbg, dbg[:, :], d_vsc[:, :])

        scopeA.close()

        if _LIMIT >= 2:
            SH2_C0 = 1920

            def sh_off(jc, qs):
                o = SH2_C0 - jc * 128 + qs * 512 + (6144 if jc >= 16 else 0)
                assert 0 <= o <= 8064 - 512
                return o

            with contextlib.ExitStack() as ph2:
                p2 = ph2.enter_context
                scT = p2(nc.sbuf_tensor("p2_scT", [128, 32 * 512], BF16))
                uT = p2(nc.sbuf_tensor("p2_uT", [128, 16 * 512], BF16))
                ucb = [p2(nc.sbuf_tensor("p2_uc0", [128, 8 * 512], BF16)),
                       p2(nc.sbuf_tensor("p2_uc1", [128, 8 * 512], BF16))]
                owb = [p2(nc.sbuf_tensor("p2_ow0", [128, 1024], BF16)),
                       p2(nc.sbuf_tensor("p2_ow1", [128, 1024], BF16))]
                shb = [p2(nc.sbuf_tensor("p2_sh0", [128, 512], F32)),
                       p2(nc.sbuf_tensor("p2_sh1", [128, 512], F32))]
                sc32 = [p2(nc.sbuf_tensor("p2_s0", [128, 512], F32)),
                        p2(nc.sbuf_tensor("p2_s1", [128, 512], F32))]
                rl32 = [p2(nc.sbuf_tensor("p2_r0", [128, 512], F32)),
                        p2(nc.sbuf_tensor("p2_r1", [128, 512], F32))]
                xnfm2 = p2(nc.sbuf_tensor("p2_xnfm", [128, 4096], BF16))
                z32t = p2(nc.sbuf_tensor("p2_z0", [128, 512], F32))
                pb = [p2(nc.psum_tensor(f"p2_pb{i}", [128, 512], F32)) for i in range(8)]

                pb_free = {i: [] for i in range(8)}

                def pb_wait(i):
                    for sem, v in pb_free[i]:
                        if v > 0:
                            emit("tensor", lambda e, s=sem, vv=v: e.wait_ge(s.h, vv))
                    pb_free[i] = []

                for engn in ("sync", "scalar"):
                    wait(engn, sXnW, xnw_all)
                    wait(engn, sSt0, st0_all)
                    wait(engn, sVs[0], vst_all[0])
                    wait(engn, sVs[1], vst_all[1])
                    wait(engn, dPE, ph1_pe)
                    wait(engn, dDVE, ph1_dve)
                    wait(engn, dACT, ph1_act)

                uT_lastpe = 0
                scT_lastpe = 0
                xnfm2_read = 0
                xs2_read = 0
                vq_last = {0: 0, 1: 0, 2: 0}
                uc_tgt = [0, 0]
                uc_lastpe = [0, 0]
                ow_tgt = [0, 0]
                ow_lastpe = [0, 0]
                sh_tgt = [0, 0]
                prev_gate_mm = 0
                prev_zfmevac = 0
                combine_done = 0

                for qs in range(QS):
                    wait("sync", dPE, xnfm2_read)
                    xf_tgt = dma("sync", sXf, xnfm2[:, :], d_xnfm[qs * 128:(qs + 1) * 128, :])

                    with contextlib.ExitStack() as svs:
                        vbuf = [svs.enter_context(nc.sbuf_tensor(f"p2v{i}_{qs}", [128, 16 * 512], BF16))
                                for i in range(3)]
                        wait("sync", dPE, prev_gate_mm)
                        wait("sync", dDVE, prev_zfmevac)
                        wait("sync", dDVE, combine_done)
                        wait("sync", sZW, sZW.v)

                        for eb in range(4):
                            usl = eb % 2
                            wait("scalar", dPE, uc_lastpe[usl])
                            uc_tgt[usl] = dma("scalar", sUc[usl],
                                              ucb[usl][:, :].rearrange("p (fc e) -> p fc e", fc=8),
                                              d_uct.rearrange("(fc p) e -> p fc e", p=128)[:, :, eb * 512:(eb + 1) * 512])
                            wait("tensor", sUc[usl], uc_tgt[usl])
                            if eb == 0:
                                wait("tensor", sXf, xf_tgt)
                            for e4 in range(4):
                                ec = eb * 4 + e4
                                psl = ec % 2
                                pb_wait(psl)
                                if ec == 0:
                                    wait("tensor", dPE, uT_lastpe)
                                for fc in range(8):
                                    op("tensor", "matmul", pb[psl][0:128, :],
                                       ucb[usl][:, fc * 512 + e4 * 128:fc * 512 + (e4 + 1) * 128],
                                       xnfm2[:, fc * 512:(fc + 1) * 512],
                                       start=(fc == 0), stop=(fc == 7),
                                       inc=dPE if fc == 7 else None)
                                wait("scalar", dPE, dPE.v)
                                op("scalar", "activation", uT[:, ec * 512:(ec + 1) * 512],
                                   pb[psl][0:128, :], AF.Silu, bias=uvbu_c[:, ec:ec + 1], inc=dACT)
                                pb_free[psl] = [(dACT, dACT.v)]
                            uc_lastpe[usl] = dPE.v
                        uT_done_act = dACT.v
                        xnfm2_read = dPE.v

                    if _P2S >= 2:
                            add_done = [0] * 32
                            relu_done = [0] * 32
                            for jc in range(32):
                                psl = jc % 2
                                if jc >= 2:
                                    wait("scalar", dDVE, add_done[jc - 2])
                                sh_tgt[psl] = dma("scalar", sSh[psl], shb[psl][:, :],
                                                  d_sh2[:, sh_off(jc, qs):sh_off(jc, qs) + 512])
                                pb_wait(psl)
                                op("tensor", "matmul", pb[psl][0:128, :],
                                   k_fm[:, jc * 128:(jc + 1) * 128], q_fm[:, qs * 512:(qs + 1) * 512],
                                   start=True, stop=True, inc=dPE)
                                mm_v = dPE.v
                                wait("vector", dPE, mm_v)
                                wait("vector", sSh[psl], sh_tgt[psl])
                                if jc >= 2:
                                    wait("vector", dACT, relu_done[jc - 2])
                                if jc == 0:
                                    wait("vector", dPE, scT_lastpe)
                                op("vector", "tensor_add", sc32[psl][:, :], pb[psl][0:128, :],
                                   shb[psl][:, :], inc=dDVE)
                                add_done[jc] = dDVE.v
                                pb_free[psl] = [(dDVE, dDVE.v)]
                                wait("scalar", dDVE, add_done[jc])
                                op("scalar", "activation", rl32[psl][:, :], sc32[psl][:, :], AF.Relu,
                                   inc=dACT)
                                relu_done[jc] = dACT.v
                                wait("vector", dACT, relu_done[jc])
                                op("vector", "tensor_mul", scT[:, jc * 512:(jc + 1) * 512],
                                   sc32[psl][:, :], rl32[psl][:, :], inc=dDVE)
                            scores_done = dDVE.v

                            gate_done = [0] * 16
                            for qd in range(4):
                                for jh in range(2):
                                    slot = (2 * qd + jh) % 3
                                    wait("sync", dPE, vq_last[slot])
                                    dma("sync", sVq[slot],
                                        vbuf[slot][:, :].rearrange("p (jc e) -> p jc e", jc=16),
                                        d_vsc.rearrange("(jc p) e -> p jc e", p=128)[
                                            :, jh * 16:(jh + 1) * 16, qd * 512:(qd + 1) * 512])
                                for e4 in range(4):
                                    ec = qd * 4 + e4
                                    psl = 2 + ec % 2
                                    pb_wait(psl)
                                    if e4 == 0 and qd == 0:
                                        wait("tensor", dDVE, scores_done)
                                        wait("tensor", dACT, uT_done_act)
                                    for jc in range(32):
                                        jh = jc // 16
                                        slot = (2 * qd + jh) % 3
                                        if jc % 16 == 0 and e4 == 0:
                                            wait("tensor", sVq[slot], sVq[slot].v)
                                        op("tensor", "matmul", pb[psl][0:128, :],
                                           vbuf[slot][:, (jc % 16) * 512 + e4 * 128:(jc % 16) * 512 + (e4 + 1) * 128],
                                           scT[:, jc * 512:(jc + 1) * 512],
                                           start=(jc == 0), stop=(jc == 31),
                                           inc=dPE if jc == 31 else None)
                                    wait("vector", dPE, dPE.v)
                                    op("vector", "tensor_tensor", uT[:, ec * 512:(ec + 1) * 512],
                                       pb[psl][0:128, :], uT[:, ec * 512:(ec + 1) * 512], ALU.mult,
                                       inc=dDVE)
                                    gate_done[ec] = dDVE.v
                                    pb_free[psl] = [(dDVE, dDVE.v)]
                                vq_last[(2 * qd) % 3] = dPE.v
                                vq_last[(2 * qd + 1) % 3] = dPE.v
                            scT_lastpe = dPE.v

                    if _P2S >= 4:
                        with contextlib.ExitStack() as sys_:
                            y1_fm = sys_.enter_context(nc.sbuf_tensor(f"p2y1f_{qs}", [128, 8 * 512], F32))
                            y1_tm = sys_.enter_context(nc.sbuf_tensor(f"p2y1t_{qs}", [128, 4096], F32))
                            zfm_sb = sys_.enter_context(nc.sbuf_tensor(f"p2zf_{qs}", [128, 4096], BF16))
                            xsup2 = sys_.enter_context(nc.sbuf_tensor(f"p2xs_{qs}", [128, 4096], F32))
                            wait("sync", dPE, scT_lastpe)
                            wait("sync", dDVE, xs2_read)
                            x2_tgt = dma("sync", sX2, xsup2[:, :].rearrange("p (tc f) -> p tc f", f=H),
                                         d_x.rearrange("(tc p) f -> p tc f", p=128)[:, qs * 4:(qs + 1) * 4, :])

                            for ec in range(16):
                                osl = ec % 2
                                wait("scalar", dPE, ow_lastpe[osl])
                                ow_tgt[osl] = dma("scalar", sOw[osl], owb[osl][:, :],
                                                  d_owt[ec * 128:(ec + 1) * 128, :])
                                wait("tensor", sOw[osl], ow_tgt[osl])
                                wait("tensor", dDVE, gate_done[ec])
                                if ec == 0:
                                    for i in range(8):
                                        pb_wait(i)
                                for hc in range(8):
                                    op("tensor", "matmul", pb[hc][0:128, :],
                                       owb[osl][:, hc * 128:(hc + 1) * 128],
                                       uT[:, ec * 512:(ec + 1) * 512],
                                       start=(ec == 0), stop=(ec == 15),
                                       inc=dPE if hc == 7 else None)
                                ow_lastpe[osl] = dPE.v
                            oproj_done = dPE.v
                            uT_lastpe = oproj_done

                            wait("scalar", dPE, oproj_done)
                            for hc in range(8):
                                op("scalar", "activation", y1_fm[:, hc * 512:(hc + 1) * 512],
                                   pb[hc][0:128, :], AF.Identity, bias=ob_c[:, hc:hc + 1],
                                   inc=dACT if hc == 7 else None)
                            y1_evac = dACT.v
                            for i in range(8):
                                pb_free[i] = [(dACT, y1_evac)]

                        if _P2S >= 5:
                                z_aff = [0] * 4
                                for tc in range(4):
                                    pA = 2 * (tc % 2)
                                    pB = pA + 1
                                    pb_wait(pA)
                                    pb_wait(pB)
                                    if tc == 0:
                                        wait("tensor", dACT, y1_evac)
                                    for hc in range(8):
                                        tgt = pb[pA] if hc < 4 else pb[pB]
                                        col = (hc % 4) * 128
                                        op("tensor", "transpose", tgt[0:128, col:col + 128],
                                           y1_fm[:, hc * 512 + tc * 128:hc * 512 + (tc + 1) * 128],
                                           idn32[:, :], inc=dPE if hc == 7 else None)
                                    tr_v = dPE.v
                                    wait("vector", dPE, tr_v)
                                    wait("vector", sX2, x2_tgt)
                                    op("vector", "tensor_add", y1_tm[:, tc * H:tc * H + 512],
                                       pb[pA][0:128, :], xsup2[:, tc * H:tc * H + 512])
                                    op("vector", "tensor_add", y1_tm[:, tc * H + 512:(tc + 1) * H],
                                       pb[pB][0:128, :], xsup2[:, tc * H + 512:(tc + 1) * H], inc=dDVE)
                                    y1tm_done = dDVE.v
                                    pb_free[pA] = [(dDVE, y1tm_done)]
                                    pb_free[pB] = [(dDVE, y1tm_done)]
                                    wait("scalar", dDVE, y1tm_done)
                                    op("scalar", "activation", sq_trash[:, :], y1_tm[:, tc * H:(tc + 1) * H],
                                       AF.Square, accum_out=stat_sq[:, tc:tc + 1], inc=dACT)
                                    sq_done = dACT.v
                                    drain("vector")
                                    op("vector", "reduce_sum", stat_sum[:, tc:tc + 1],
                                       y1_tm[:, tc * H:(tc + 1) * H], AX.X)
                                    drain("vector")
                                    op("vector", "tensor_scalar_mul", mu4[:, tc:tc + 1], stat_sum[:, tc:tc + 1], 1.0 / H)
                                    drain("vector")
                                    op("vector", "tensor_mul", musq4[:, tc:tc + 1], mu4[:, tc:tc + 1], mu4[:, tc:tc + 1])
                                    drain("vector")
                                    wait("vector", dACT, sq_done)
                                    op("vector", "tensor_scalar", var4[:, tc:tc + 1], stat_sq[:, tc:tc + 1],
                                       1.0 / H, None, ALU.mult)
                                    drain("vector")
                                    op("vector", "tensor_sub", var4[:, tc:tc + 1], var4[:, tc:tc + 1], musq4[:, tc:tc + 1])
                                    drain("vector")
                                    op("vector", "tensor_scalar", var4[:, tc:tc + 1], var4[:, tc:tc + 1],
                                       float(EPS), None, ALU.add, inc=dDVE)
                                    wait("scalar", dDVE, dDVE.v)
                                    op("scalar", "activation", var4[:, tc:tc + 1], var4[:, tc:tc + 1], AF.Ln)
                                    drain("scalar")
                                    op("scalar", "activation", rstd4[:, tc:tc + 1], var4[:, tc:tc + 1],
                                       AF.Exp, scale=-0.5, inc=dACT)
                                    wait("vector", dACT, dACT.v)
                                    op("vector", "scalar_tensor_tensor", nmr4[:, tc:tc + 1], mu4[:, tc:tc + 1],
                                       -1.0, rstd4[:, tc:tc + 1], op0=ALU.mult, op1=ALU.mult, inc=dDVE)
                                    wait("scalar", dDVE, dDVE.v)
                                    op("scalar", "activation", y1_tm[:, tc * H:(tc + 1) * H],
                                       y1_tm[:, tc * H:(tc + 1) * H], AF.Identity,
                                       scale=rstd4[:, tc:tc + 1], bias=nmr4[:, tc:tc + 1], inc=dACT)
                                    z_aff[tc] = dACT.v
                                    wait("sync", dACT, z_aff[tc])
                                    tt = qs * 4 + tc
                                    dma("sync", sZW, d_ztm[tt * 128:(tt + 1) * 128, :],
                                        y1_tm[:, tc * H:(tc + 1) * H])
                                xs2_read = dDVE.v

                        zf_ev_v = [0] * 8
                        zf_ev_a = [0] * 8
                        if _P2S >= 6:
                            for fc in range(8):
                                psl = 4 + fc % 2
                                pb_wait(psl)
                                if fc == 0:
                                    wait("tensor", dACT, z_aff[3])
                                for tc in range(4):
                                    op("tensor", "transpose", pb[psl][0:128, tc * 128:(tc + 1) * 128],
                                       y1_tm[:, tc * H + fc * 128:tc * H + (fc + 1) * 128], idn32[:, :],
                                       inc=dPE if tc == 3 else None)
                                trz_v = dPE.v
                                wait("vector", dPE, trz_v)
                                op("vector", "tensor_copy", zfm_sb[:, fc * 512:(fc + 1) * 512],
                                   pb[psl][0:128, :], inc=dDVE)
                                zf_ev_v[fc] = dDVE.v
                                wait("scalar", dDVE, zf_ev_v[fc])
                                op("scalar", "activation", y1_fm[:, fc * 512:(fc + 1) * 512],
                                   pb[psl][0:128, :], AF.Copy, inc=dACT)
                                zf_ev_a[fc] = dACT.v
                                pb_free[psl] = [(dDVE, zf_ev_v[fc]), (dACT, zf_ev_a[fc])]
                            wait("sync", dDVE, zf_ev_v[7])
                            prev_zfmevac = zf_ev_v[7]
                            if not _NOZFMST:
                                dma("sync", sZW,
                                    d_zfm.rearrange("p (fc t) -> p fc t", fc=8)[:, :, qs * 512:(qs + 1) * 512],
                                    zfm_sb[:, :].rearrange("p (fc t) -> p fc t", fc=8))

                        gate_mm = 0
                        if _P2S >= 7:
                            wait("tensor", dACT, zf_ev_a[7])
                            pb_wait(6)
                            for fc in range(8):
                                for tc in range(4):
                                    op("tensor", "matmul", pb[6][0:128, tc * 8:(tc + 1) * 8],
                                       y1_fm[:, fc * 512 + tc * 128:fc * 512 + (tc + 1) * 128],
                                       gw_sb[:, fc * 8:(fc + 1) * 8],
                                       start=(fc == 0 and tc == 0), stop=False)
                            for tc in range(4):
                                op("tensor", "matmul", pb[6][0:128, tc * 8:(tc + 1) * 8],
                                   ones_row[0:1, 0:128], gb_row[0:1, 0:8],
                                   start=False, stop=(tc == 3), inc=dPE if tc == 3 else None)
                            gate_mm = dPE.v
                            prev_gate_mm = gate_mm

                        if _P2S >= 8:
                                c_mx = mu4
                                c_t = musq4
                                e8 = sq_trash
                                for tc in range(4):
                                    tt = qs * 4 + tc
                                    pg = pb[6][0:128, tc * 8:(tc + 1) * 8]
                                    wait("vector", dPE, gate_mm)
                                    op("vector", "tensor_reduce", c_mx[:, 0:1], pg, AX.X, ALU.max)
                                    drain("vector")
                                    op("vector", "tensor_scalar_mul", c_mx[:, 1:2], c_mx[:, 0:1], -1.0,
                                       inc=dDVE)
                                    wait("scalar", dDVE, dDVE.v)
                                    op("scalar", "activation", e8[:, 0:8], pg, AF.Exp,
                                       bias=c_mx[:, 1:2], accum_out=c_mx[:, 2:3], inc=dACT)
                                    wait("vector", dACT, dACT.v)
                                    op("vector", "tensor_reduce", c_t[:, 0:1], e8[:, 0:8], AX.X, ALU.max)
                                    drain("vector")
                                    op("vector", "tensor_single_scalar", e8[:, 8:16], e8[:, 0:8],
                                       c_t[:, 0:1], ALU.is_equal)
                                    drain("vector")
                                    op("vector", "scalar_tensor_tensor", e8[:, 16:24], e8[:, 8:16], -BIG,
                                       iota8b[:, :], op0=ALU.mult, op1=ALU.add)
                                    drain("vector")
                                    op("vector", "tensor_reduce", c_t[:, 1:2], e8[:, 16:24], AX.X, ALU.min)
                                    drain("vector")
                                    op("vector", "tensor_single_scalar", e8[:, 24:32], iota8[:, :],
                                       c_t[:, 1:2], ALU.is_equal)
                                    drain("vector")
                                    op("vector", "scalar_tensor_tensor", e8[:, 32:40], e8[:, 24:32], -BIG,
                                       e8[:, 0:8], op0=ALU.mult, op1=ALU.add)
                                    drain("vector")
                                    op("vector", "tensor_reduce", c_t[:, 2:3], e8[:, 32:40], AX.X, ALU.max)
                                    drain("vector")
                                    op("vector", "tensor_single_scalar", e8[:, 40:48], e8[:, 32:40],
                                       c_t[:, 2:3], ALU.is_equal)
                                    drain("vector")
                                    op("vector", "scalar_tensor_tensor", e8[:, 48:56], e8[:, 40:48], -BIG,
                                       iota8b[:, :], op0=ALU.mult, op1=ALU.add)
                                    drain("vector")
                                    op("vector", "tensor_reduce", c_t[:, 3:4], e8[:, 48:56], AX.X, ALU.min)
                                    drain("vector")
                                    op("vector", "tensor_single_scalar", e8[:, 56:64], iota8[:, :],
                                       c_t[:, 3:4], ALU.is_equal)
                                    drain("vector")
                                    op("vector", "tensor_add", e8[:, 64:72], e8[:, 24:32], e8[:, 56:64])
                                    drain("vector")
                                    op("vector", "tensor_mul", e8[:, 72:80], e8[:, 0:8], e8[:, 64:72])
                                    drain("vector")
                                    op("vector", "reduce_sum", c_mx[:, 3:4], e8[:, 72:80], AX.X)
                                    drain("vector")
                                    op("vector", "reciprocal", c_t[:, 0:1], c_mx[:, 3:4])
                                    drain("vector")
                                    op("vector", "tensor_scalar_mul", combine[:, tt * 8:(tt + 1) * 8],
                                       e8[:, 72:80], c_t[:, 0:1], inc=dDVE)
                                combine_done = dDVE.v
                                pb_free[6] = [(dDVE, combine_done)]

                zW_all = sZW.v
                ph2_pe, ph2_dve, ph2_act = dPE.v, dDVE.v, dACT.v

                if _DEBUG == "z":
                    wait("sync", sZW, zW_all)
                    dma("sync", sDbg, dbg[:, :], d_ztm[:, :])
                if _DEBUG == "zfm":
                    wait("sync", sZW, zW_all)
                    dma("sync", sDbg, dbg[:, :], d_zfm[:, :])

        if _LIMIT >= 3:
            with contextlib.ExitStack() as ph3:
                p3 = ph3.enter_context
                zfm3 = p3(nc.sbuf_tensor("p3_zfm", [128, 8 * OWN], BF16))
                y2 = p3(nc.sbuf_tensor("p3_y2", [128, 16 * H], F32))
                w1b = [p3(nc.sbuf_tensor("p3_w1a", [128, 4096], BF16)),
                       p3(nc.sbuf_tensor("p3_w1b", [128, 4096], BF16))]
                w3b = [p3(nc.sbuf_tensor("p3_w3a", [128, 4096], BF16)),
                       p3(nc.sbuf_tensor("p3_w3b", [128, 4096], BF16))]
                w2b = [p3(nc.sbuf_tensor("p3_w2a", [128, 4096], BF16)),
                       p3(nc.sbuf_tensor("p3_w2b", [128, 4096], BF16))]
                ht = [p3(nc.sbuf_tensor("p3_ht0", [128, 256], F32)),
                      p3(nc.sbuf_tensor("p3_ht1", [128, 256], F32))]
                hbb = [p3(nc.sbuf_tensor("p3_hb0", [128, 256], BF16)),
                       p3(nc.sbuf_tensor("p3_hb1", [128, 256], BF16))]
                ph1_ = [p3(nc.psum_tensor("p3_h1a", [128, 512], F32)),
                        p3(nc.psum_tensor("p3_h1b", [128, 512], F32))]
                ph3_ = [p3(nc.psum_tensor("p3_h3a", [128, 512], F32)),
                        p3(nc.psum_tensor("p3_h3b", [128, 512], F32))]
                poe = [p3(nc.psum_tensor("p3_oe0", [128, 1024], F32)),
                       p3(nc.psum_tensor("p3_oe1", [128, 1024], F32))]

                for engn in ("sync", "gpsimd"):
                    wait(engn, sZW, zW_all)
                    wait(engn, dPE, ph2_pe)
                    wait(engn, dDVE, ph2_dve)
                    wait(engn, dACT, ph2_act)
                dma("sync", sZR, zfm3[:, :], d_zfm[:, :])
                dma("sync", sZR, y2[:, :].rearrange("p (tt d) -> p tt d", d=H),
                    d_ztm.rearrange("(tt p) d -> p tt d", p=128))
                zr_all = sZR.v

                groupdone = [0] * 64
                mw_tgt = [0, 0]
                hmul_done = {}
                w2_done = {}
                u_lin = 0
                oeevac_done = 0
                first = True

                for idx in range(64):
                    e_, g_ = idx // 8, idx % 8
                    par = idx % 2
                    if idx >= 2:
                        wait("gpsimd", dPE, groupdone[idx - 2])
                    dma("gpsimd", sMW[par],
                        w1b[par][:, :].rearrange("p (dc f) -> p dc f", dc=8),
                        d_w1.rearrange("(r p) f -> p r f", p=128)[:, e_ * 8:(e_ + 1) * 8,
                                                                 g_ * 512:(g_ + 1) * 512])
                    dma("gpsimd", sMW[par],
                        w3b[par][:, :].rearrange("p (dc f) -> p dc f", dc=8),
                        d_w3.rearrange("(r p) f -> p r f", p=128)[:, e_ * 8:(e_ + 1) * 8,
                                                                 g_ * 512:(g_ + 1) * 512])
                    mw_tgt[par] = dma("gpsimd", sMW[par],
                                      w2b[par][:, :].rearrange("p (sc d) -> p sc d", sc=4),
                                      d_w2.rearrange("(r p) d -> p r d", p=128)[
                                          :, e_ * 32 + g_ * 4:e_ * 32 + (g_ + 1) * 4, :])
                    wait("tensor", sMW[par], mw_tgt[par])
                    if first:
                        wait("tensor", sZR, zr_all)
                        first = False

                    def h_mms(ts, sc, u):
                        pp = sc % 2
                        wait("tensor", dDVE, hmul_done.get(u - 2, 0))
                        for dc in range(8):
                            op("tensor", "matmul", ph1_[pp][0:128, 0:256],
                               w1b[par][:, dc * 512 + sc * 128:dc * 512 + (sc + 1) * 128],
                               zfm3[:, dc * OWN + ts * 256:dc * OWN + (ts + 1) * 256],
                               start=(dc == 0), stop=(dc == 7))
                        for dc in range(8):
                            op("tensor", "matmul", ph3_[pp][0:128, 0:256],
                               w3b[par][:, dc * 512 + sc * 128:dc * 512 + (sc + 1) * 128],
                               zfm3[:, dc * OWN + ts * 256:dc * OWN + (ts + 1) * 256],
                               start=(dc == 0), stop=(dc == 7),
                               inc=dPE if dc == 7 else None)
                        hv = dPE.v
                        wait("scalar", dPE, hv)
                        wait("scalar", dDVE, hmul_done.get(u - 2, 0))
                        op("scalar", "activation", ht[pp][:, :], ph1_[pp][0:128, 0:256], AF.Silu,
                           inc=dACT)
                        wait("vector", dACT, dACT.v)
                        wait("vector", dPE, w2_done.get(u - 2, 0))
                        op("vector", "tensor_tensor", hbb[pp][:, :], ht[pp][:, :],
                           ph3_[pp][0:128, 0:256], ALU.mult, inc=dDVE)
                        hmul_done[u] = dDVE.v

                    def w2_mms(ts, sc, u, gate_oe):
                        pp = sc % 2
                        wait("tensor", dDVE, hmul_done[u])
                        if gate_oe:
                            wait("tensor", dDVE, gate_oe)
                        for tc2 in range(2):
                            for dh in range(2):
                                op("tensor", "matmul", poe[tc2][0:128, dh * 512:(dh + 1) * 512],
                                   hbb[pp][:, tc2 * 128:(tc2 + 1) * 128],
                                   w2b[par][:, sc * 1024 + dh * 512:sc * 1024 + (dh + 1) * 512],
                                   start=(sc == 0), stop=(sc == 3),
                                   inc=dPE if (tc2 == 1 and dh == 1) else None)
                        w2_done[u] = dPE.v

                    for ts in range(8):
                        u0 = u_lin
                        h_mms(ts, 0, u0 + 0)
                        h_mms(ts, 1, u0 + 1)
                        w2_mms(ts, 0, u0 + 0, oeevac_done)
                        h_mms(ts, 2, u0 + 2)
                        w2_mms(ts, 1, u0 + 1, 0)
                        h_mms(ts, 3, u0 + 3)
                        w2_mms(ts, 2, u0 + 2, 0)
                        w2_mms(ts, 3, u0 + 3, 0)
                        u_lin += 4
                        oe_v = dPE.v
                        wait("vector", dPE, oe_v)
                        for tc2 in range(2):
                            tt = ts * 2 + tc2
                            op("vector", "scalar_tensor_tensor", y2[:, tt * H:(tt + 1) * H],
                               poe[tc2][0:128, :], combine[:, tt * 8 + e_:tt * 8 + e_ + 1],
                               y2[:, tt * H:(tt + 1) * H], op0=ALU.mult, op1=ALU.add,
                               inc=dDVE if tc2 == 1 else None)
                        oeevac_done = dDVE.v
                    groupdone[idx] = dPE.v
                moe_done_dve = dDVE.v

                if _DEBUG == "y2":
                    wait("sync", dDVE, moe_done_dve)
                    dma("sync", sDbg, dbg.rearrange("(tt p) d -> p tt d", p=128),
                        y2[:, :].rearrange("p (tt d) -> p tt d", d=H))
                if _DEBUG == "comb":
                    wait("sync", dDVE, moe_done_dve)
                    dma("sync", sDbg, dbg[:, :], combine[:, :])

                outb = [p3(nc.sbuf_tensor("p4_o0", [128, 1024], F32)),
                        p3(nc.sbuf_tensor("p4_o1", [128, 1024], F32))]
                out_tgt = [0, 0]
                wait("scalar", dDVE, moe_done_dve)
                wait("vector", dDVE, moe_done_dve)
                for tt in range(16):
                    tc = tt % 4
                    osl = tt % 2
                    op("scalar", "activation", sq_trash[:, :], y2[:, tt * H:(tt + 1) * H],
                       AF.Square, accum_out=stat_sq[:, tc:tc + 1], inc=dACT)
                    sq_done = dACT.v
                    drain("vector")
                    op("vector", "reduce_sum", stat_sum[:, tc:tc + 1], y2[:, tt * H:(tt + 1) * H], AX.X)
                    drain("vector")
                    op("vector", "tensor_scalar_mul", mu4[:, tc:tc + 1], stat_sum[:, tc:tc + 1], 1.0 / H)
                    drain("vector")
                    op("vector", "tensor_mul", musq4[:, tc:tc + 1], mu4[:, tc:tc + 1], mu4[:, tc:tc + 1])
                    drain("vector")
                    wait("vector", dACT, sq_done)
                    op("vector", "tensor_scalar", var4[:, tc:tc + 1], stat_sq[:, tc:tc + 1],
                       1.0 / H, None, ALU.mult)
                    drain("vector")
                    op("vector", "tensor_sub", var4[:, tc:tc + 1], var4[:, tc:tc + 1], musq4[:, tc:tc + 1])
                    drain("vector")
                    op("vector", "tensor_scalar", var4[:, tc:tc + 1], var4[:, tc:tc + 1],
                       float(EPS), None, ALU.add, inc=dDVE)
                    wait("scalar", dDVE, dDVE.v)
                    op("scalar", "activation", var4[:, tc:tc + 1], var4[:, tc:tc + 1], AF.Ln)
                    drain("scalar")
                    op("scalar", "activation", rstd4[:, tc:tc + 1], var4[:, tc:tc + 1],
                       AF.Exp, scale=-0.5, inc=dACT)
                    wait("vector", dACT, dACT.v)
                    op("vector", "scalar_tensor_tensor", nmr4[:, tc:tc + 1], mu4[:, tc:tc + 1],
                       -1.0, rstd4[:, tc:tc + 1], op0=ALU.mult, op1=ALU.mult, inc=dDVE)
                    wait("scalar", dDVE, dDVE.v)
                    wait("scalar", sOut, out_tgt[osl])
                    op("scalar", "activation", outb[osl][:, :], y2[:, tt * H:(tt + 1) * H],
                       AF.Identity, scale=rstd4[:, tc:tc + 1], bias=nmr4[:, tc:tc + 1], inc=dACT)
                    wait("sync", dACT, dACT.v)
                    out_tgt[osl] = dma("sync", sOut, d_out[tt * 128:(tt + 1) * 128, :],
                                       outb[osl][:, :])

                wait("sync", sOut, sOut.v)
                if _DEBUG is not None and sDbg.v:
                    wait("sync", sDbg, sDbg.v)

        for s_ in sems:
            wait("sync", s_, s_.v)

        with contextlib.ExitStack() as semstack:
            for s in sems:
                s.h = semstack.enter_context(nc.semaphore(s.name))
            with nc.Block() as block:
                @block.sync
                def _(eng):
                    for f in prog["sync"]:
                        f(eng)

                @block.scalar
                def _(eng):
                    for f in prog["scalar"]:
                        f(eng)

                @block.vector
                def _(eng):
                    for f in prog["vector"]:
                        f(eng)

                @block.tensor
                def _(eng):
                    for f in prog["tensor"]:
                        f(eng)

                @block.gpsimd
                def _(eng):
                    for f in prog["gpsimd"]:
                        f(eng)
    return nc



def _rope_tables():
    inv = np.array(_INV_FREQ_BITS, dtype=np.uint32).view(np.float32)
    pos = np.arange(T, dtype=np.float32)
    args = (pos[:, None] * inv[None, :]).astype(np.float32)
    return np.sin(args).astype(np.float32), np.cos(args).astype(np.float32)


def _beta(w, d):
    r = d + 511
    r = np.where(r < 0, r + 1023, r)
    return w[np.clip(r, 0, 1022)]


def _sh2ext(w, h):
    p = np.arange(128)[:, None]
    m = np.arange(8064)[None, :]
    da = p - m + 1920
    db = p - m + 1920 + 6144 - 4096 * h
    return np.where(m < 3968, _beta(w, da), _beta(w, db)).astype(np.float32)


def kernel(L0_attn_x, L0_attn_ln_w, L0_attn_ln_b, L0_attn_uv_w, L0_attn_uv_b,
           L0_attn_gamma, L0_attn_beta, L0_attn_w, L0_attn_o_w, L0_attn_o_b,
           L0_ffn_gate_w, L0_ffn_gate_b, L0_ffn_w1, L0_ffn_w3, L0_ffn_w2):
    import ml_dtypes
    x = np.ascontiguousarray(np.asarray(L0_attn_x, dtype=np.float32))
    uv_w = np.ascontiguousarray(np.asarray(L0_attn_uv_w, dtype=np.float32))
    uv_b = np.ascontiguousarray(np.asarray(L0_attn_uv_b, dtype=np.float32))
    ln_w = np.ascontiguousarray(np.asarray(L0_attn_ln_w, dtype=np.float32)).reshape(H, 1)
    ln_b = np.ascontiguousarray(np.asarray(L0_attn_ln_b, dtype=np.float32)).reshape(H, 1)
    gamma = np.asarray(L0_attn_gamma, dtype=np.float32)
    beta = np.asarray(L0_attn_beta, dtype=np.float32)
    w = np.ascontiguousarray(np.asarray(L0_attn_w, dtype=np.float32))
    o_w = np.ascontiguousarray(np.asarray(L0_attn_o_w, dtype=np.float32))
    o_b = np.ascontiguousarray(np.asarray(L0_attn_o_b, dtype=np.float32)).reshape(H, 1)
    gate_w = np.ascontiguousarray(np.asarray(L0_ffn_gate_w, dtype=np.float32))
    gate_b = np.ascontiguousarray(np.asarray(L0_ffn_gate_b, dtype=np.float32)).reshape(1, 8)
    w1 = np.ascontiguousarray(np.asarray(L0_ffn_w1, dtype=np.float32)).reshape(8 * H, FFN)
    w3 = np.ascontiguousarray(np.asarray(L0_ffn_w3, dtype=np.float32)).reshape(8 * H, FFN)
    w2 = np.ascontiguousarray(np.asarray(L0_ffn_w2, dtype=np.float32)).reshape(8 * FFN, H)

    sin_tm, cos_tm = _rope_tables()
    shared = dict(
        uv_w=uv_w, uv_b_row=uv_b.reshape(1, UVC),
        uv_b_col=np.ascontiguousarray(uv_b.reshape(UVC, 1)),
        ln_w=ln_w, ln_b=ln_b,
        gamma0=np.ascontiguousarray(gamma[0].reshape(S, 1)),
        gamma1=np.ascontiguousarray(gamma[1].reshape(S, 1)),
        beta0=np.ascontiguousarray(beta[0].reshape(S, 1)),
        beta1=np.ascontiguousarray(beta[1].reshape(S, 1)),
        o_w=o_w, o_b=o_b, gate_w=gate_w, gate_b=gate_b,
        w1=w1, w3=w3, w2=w2,
        idn32=np.eye(128, dtype=np.float32),
        idn16=np.eye(128).astype(ml_dtypes.bfloat16),
        ones_row=np.ones((1, 128), np.float32),
        iota8=np.tile(np.arange(8, dtype=np.float32)[None, :], (128, 1)),
        iota8b=np.tile(np.arange(8, dtype=np.float32)[None, :], (128, 1)) + BIG,
    )
    sh2 = [_sh2ext(w, hh) for hh in (0, 1)]

    in_maps = []
    for c in range(8):
        b, hh = c // 2, c % 2
        own = x[b, 2048 * hh:2048 * hh + 2048]
        oth = x[b, 2048 * (1 - hh):2048 * (1 - hh) + 2048]
        x_p = np.ascontiguousarray(np.concatenate([own, oth], axis=0))
        perm = np.concatenate([np.arange(2048 * hh, 2048 * hh + 2048),
                               np.arange(2048 * (1 - hh), 2048 * (1 - hh) + 2048)])
        m = dict(shared)
        m["x"] = x_p
        m["sh2ext"] = sh2[hh]
        m["sin_tm"] = np.ascontiguousarray(sin_tm[perm])
        m["cos_tm"] = np.ascontiguousarray(cos_tm[perm])
        in_maps.append(m)

    nc = build_program()
    res = run_bass_kernel_spmd(nc, in_maps, core_ids=list(range(8)))

    out = np.empty((4, 4096, 1024), np.float32)
    for c in range(8):
        b, hh = c // 2, c % 2
        out[b, 2048 * hh:2048 * hh + 2048] = res.results[c]["out"]
    if _DEBUG is not None:
        kernel._dbg = [r.get("dbg") for r in res.results]
    return out
